# version 17
# baseline (speedup 1.0000x reference)
import math
import numpy as np

import concourse.bass as bass
import concourse.mybir as mybir
from concourse.bass_utils import run_bass_kernel_spmd

F32 = mybir.dt.float32
BF16 = mybir.dt.bfloat16
AF = mybir.ActivationFunctionType
ALU = mybir.AluOpType

NCORES = 8
NB = 2
L = 2048
D = 1024
G = 8
N = 16
R = 64
LC = 512
NLC = L // LC
NLT = LC // 128
LQ = 512
NLQ = L // LQ

C_LO = math.log(-math.log(0.999))
C_HI = math.log(10.0)
EPS = 1e-5

_TAPS = ()
_PHASES = "PAC"


class Ctr:
    __slots__ = ("h", "n")

    def __init__(self):
        self.h = None
        self.n = 0


class Prog:
    def __init__(self):
        self.items = []

    def op(self, eng, fn, waits=(), inc=None):
        ws = []
        for (c, v) in waits:
            if v > 0:
                assert v <= c.n + 10**9
                ws.append((c, v))
        self.items.append((eng, fn, ws, inc))

    def dma(self, eng, ctr, out, in_, waits=()):
        def f(e, out=out, in_=in_):
            return e.dma_start(out=out, in_=in_)
        ctr.n += 16
        self.op(eng, f, waits, (ctr, 16))
        return ctr.n

    def comp(self, eng, ctr, fn, waits=()):
        ctr.n += 1
        self.op(eng, fn, waits, (ctr, 1))
        return ctr.n


class SBAlloc:
    def __init__(self, nc, base=0):
        self.nc = nc
        self.off = base

    def alloc(self, name, shape, dtype):
        nbytes = int(np.prod(shape[1:])) * mybir.dt.size(dtype)
        off = (self.off + 31) & ~31
        t = self.nc.alloc_sbuf_tensor_at(name, list(shape), dtype, offset=off)
        self.off = off + nbytes
        return t


def build_program(taps=(), phases=None):
    phases = phases or _PHASES
    csub = 3
    if phases and phases[-1].isdigit():
        csub = int(phases[-1])
        phases = phases[:-1]
    nc = bass.Bass("TRN2", target_bir_lowering=False, debug=False)

    def kind(nm):
        return "ExternalOutput" if nm in taps else "Internal"

    x_d = nc.dram_tensor("x", [NB, L, D], F32, kind="ExternalInput")
    out_d = nc.dram_tensor("out", [NB, L, D], F32, kind="ExternalOutput")
    wrT_d = nc.dram_tensor("wrT", [D, D], F32, kind="ExternalInput")
    wkT_d = nc.dram_tensor("wkT", [D, D], F32, kind="ExternalInput")
    wvT_d = nc.dram_tensor("wvT", [D, D], F32, kind="ExternalInput")
    wdt1T_d = nc.dram_tensor("wdt1T", [D, R], F32, kind="ExternalInput")
    wdt2T_d = nc.dram_tensor("wdt2T", [R, D], F32, kind="ExternalInput")
    wbT_d = nc.dram_tensor("wbT", [D, N], F32, kind="ExternalInput")
    wcT_d = nc.dram_tensor("wcT", [D, N], F32, kind="ExternalInput")
    alog_d = nc.dram_tensor("alogp", [128, G * N], F32, kind="ExternalInput")
    dcol_d = nc.dram_tensor("dcolp", [128, G], F32, kind="ExternalInput")
    bdt_d = nc.dram_tensor("bdtp", [128, G], F32, kind="ExternalInput")
    cb_d = nc.dram_tensor("cbp", [128, G], F32, kind="ExternalInput")
    mur_d = nc.dram_tensor("murp", [128, G], F32, kind="ExternalInput")
    muk_d = nc.dram_tensor("mukp", [128, G], F32, kind="ExternalInput")
    convw_d = nc.dram_tensor("convwp", [128, G * 4], F32, kind="ExternalInput")
    convdg_d = nc.dram_tensor("convdg", [128, G * 4 * 128], F32, kind="ExternalInput")
    ones_d = nc.dram_tensor("onesr", [1, 128], F32, kind="ExternalInput")
    kconst_d = nc.dram_tensor("kconst", [128, 4], F32, kind="ExternalInput")
    ident_d = nc.dram_tensor("identm", [128, 128], F32, kind="ExternalInput")

    xcv_sc = nc.dram_tensor("xcv", [NB, G, 128, L], F32, kind=kind("xcv"))
    opart_sc = nc.dram_tensor("opart", [NB, L, D], F32, kind=kind("opart"))
    b_sc = nc.dram_tensor("bsc", [NB, N, L], F32, kind=kind("bsc"))
    c_sc = nc.dram_tensor("csc", [NB, N, L], F32, kind=kind("csc"))
    wsc = {nm: nc.dram_tensor(nm, [G, 128, D], BF16, kind="Internal")
           for nm in ("wr1s", "wr2s", "wk1s", "wk2s", "wvs")}
    xnw_sc = nc.dram_tensor("xnwd", [NB, NLC, 128, G * (LC + 4)], F32, kind=kind("xnwd"))
    stat_sc = nc.dram_tensor("statd", [NB, NLC, 128, 5 * NLT], F32, kind=kind("statd"))

    base0 = (nc.SBUF_PARTITION_SIZE_BYTES - nc.sbuf_bytes_remaining + 31) & ~31
    al = SBAlloc(nc, base0)
    ident_sb = al.alloc("identsb", [128, 128], F32)
    kconst_sb = al.alloc("kconstsb", [128, 4], F32)
    ones_sb = al.alloc("onessb", [1, 128], F32)
    alog_sb = al.alloc("alogsb", [128, G * N], F32)
    A_sb = al.alloc("Asb", [128, G * N], F32)
    lo_sb = al.alloc("losb", [128, G * N], F32)
    hi_sb = al.alloc("hisb", [128, G * N], F32)
    dcol_sb = al.alloc("dcolsb", [128, G], F32)
    bdt_sb = al.alloc("bdtsb", [128, G], F32)
    cb_sb = al.alloc("cbsb", [128, G], F32)
    mur_sb = al.alloc("mursb", [128, G], F32)
    muk_sb = al.alloc("muksb", [128, G], F32)
    omur_sb = al.alloc("omursb", [128, G], F32)
    omuk_sb = al.alloc("omuksb", [128, G], F32)
    convw_sb = al.alloc("convwsb", [128, G * 4], F32)
    cdiag_sb = al.alloc("cdiagsb", [128, G * 4 * 128], F32)
    wdt1_bf = al.alloc("wdt1bf", [128, G * R], BF16)
    wdt2_bf = al.alloc("wdt2bf", [64, D], BF16)
    wb_bf = al.alloc("wbbf", [128, G * N], BF16)
    wc_bf = al.alloc("wcbf", [128, G * N], BF16)
    zbf_sb = al.alloc("zbfsb", [64, NB * L], BF16)
    hcarry = al.alloc("hcarry", [128, NB * G * N], F32)
    vset_sb = al.alloc("vsetsb", [128, 32], F32)
    sset_sb = al.alloc("ssetsb", [128, 32], F32)
    pbase = al.off

    ab = SBAlloc(nc, pbase)
    slotRP = ab.alloc("slotRP", [128, 2 * G * D], BF16)
    slotKV = ab.alloc("slotKV", [128, 2 * G * D], BF16)
    xnw = ab.alloc("xnw", [128, G, LC + 4], F32)
    xnw_bf = ab.alloc("xnwbf", [128, G, LC + 1], BF16)
    x_ld = ab.alloc("xld", [128, 2 * NLT * D], F32)
    xn_t = ab.alloc("xnt", [128, D], F32)
    st6 = ab.alloc("st6", [128, 12], F32)
    mv = ab.alloc("mv", [128, 2 * 4 * NLT], F32)
    sc2 = ab.alloc("sc2", [128, 3 * NLT], F32)
    rstd = ab.alloc("rstd", [128, 2 * NLT], F32)
    xcv_g = ab.alloc("xcvg", [128, 3 * LC], F32)
    xcv_bf = ab.alloc("xcvbf", [128, G * LC], BF16)
    rtmp = ab.alloc("rtmp", [128, LC], F32)
    kr_sb = ab.alloc("krsb", [128, LC], F32)
    ka_bf = ab.alloc("kabf", [128, G * LC], BF16)
    ffn_g = ab.alloc("ffng", [128, LC], F32)
    opart_t = ab.alloc("opartt", [128, NLT * D], F32)
    bev_sb = ab.alloc("bevsb", [16, LC], F32)
    cev_sb = ab.alloc("cevsb", [16, LC], F32)
    zev_sb = ab.alloc("zevsb", [64, LC], F32)
    ab_end = ab.off

    cc = SBAlloc(nc, pbase)
    dt_sb = cc.alloc("dtsb", [128, G * LQ], F32)
    cx_sb = cc.alloc("cxsb", [128, G * LQ], F32)
    yacc_sb = cc.alloc("yaccsb", [128, G * LQ], F32)
    xcv_c = cc.alloc("xcvc", [128, G * LQ], F32)
    bcf_sb = cc.alloc("bcfsb", [1, 2 * 2 * 4 * LQ], F32)
    Bb_sb = cc.alloc("Bbsb", [128, 2 * LQ], F32)
    Cb_sb = cc.alloc("Cbsb", [128, 2 * LQ], F32)
    h_sb = cc.alloc("hsb", [128, 2 * LQ], F32)
    dtc_sb = cc.alloc("dtcsb", [128, 2 * LQ], F32)
    abar_sb = cc.alloc("abarsb", [128, 2 * LQ], F32)
    hc_sb = cc.alloc("hcsb", [128, 2 * LQ], F32)
    bx_sb = cc.alloc("bxsb", [128, G * LQ], F32)
    sp1 = cc.alloc("sp1", [128, LQ], F32)
    sp2 = cc.alloc("sp2", [128, LQ], F32)
    sp3 = cc.alloc("sp3", [128, LQ], F32)
    dx_sb = cc.alloc("dxsb", [128, LQ], F32)
    ostg = cc.alloc("ostg", [128, 2 * LQ], F32)
    opld = cc.alloc("opld", [128, 2 * LQ], F32)
    c_end = cc.off

    wp = SBAlloc(nc, pbase)
    wstage = wp.alloc("wstage", [128, D], F32)
    wprep = wp.alloc("wprep", [128, 2 * D], BF16)
    total = max(ab_end, c_end, wp.off)
    assert total <= nc.SBUF_PARTITION_SIZE_BYTES, f"SBUF over budget: {total}"

    pband = [nc.alloc_psum_tensor(f"pband{i}", [128, 512], F32) for i in range(8)]

    prog = Prog()
    c_ld = Ctr()
    c_st = Ctr()
    c_v = Ctr()
    c_s = Ctr()
    c_t = Ctr()
    c_p = Ctr()
    ctrs = [c_ld, c_st, c_v, c_s, c_t, c_p]
    dpool = [Ctr() for _ in range(32)]
    _dpi = [0]

    def dbatch():
        c = dpool[_dpi[0] % len(dpool)]
        _dpi[0] += 1
        return c

    def vsettle():
        return prog.comp("vector", c_v, lambda e: e.memset(vset_sb[:], 0.0))

    def ssettle():
        return prog.comp("scalar", c_s, lambda e: e.activation(
            sset_sb[:, 0:1], sset_sb[:, 0:1], AF.Copy, bias=0.0, scale=1.0))

    prog.comp("scalar", c_s, lambda e: e.memzero(sset_sb[:]))
    b_consts = dbatch()
    for dst, src in [(ident_sb[:], ident_d[:]), (ones_sb[:], ones_d[:]),
                     (alog_sb[:], alog_d[:]), (dcol_sb[:], dcol_d[:]),
                     (bdt_sb[:], bdt_d[:]), (cb_sb[:], cb_d[:]),
                     (mur_sb[:], mur_d[:]), (muk_sb[:], muk_d[:]),
                     (convw_sb[:], convw_d[:]), (kconst_sb[:], kconst_d[:]),
                     (cdiag_sb[:], convdg_d[:])]:
        prog.dma("sync", b_consts, dst, src)
    m_consts = b_consts.n
    mA = prog.comp("scalar", c_s, lambda e: e.activation(A_sb[:], alog_sb[:], AF.Exp, bias=0.0, scale=1.0),
                   waits=[(b_consts, m_consts)])
    prog.comp("scalar", c_s, lambda e: e.activation(lo_sb[:], alog_sb[:], AF.Exp, bias=kconst_sb[:, 0:1], scale=-1.0))
    prog.comp("scalar", c_s, lambda e: e.activation(hi_sb[:], alog_sb[:], AF.Exp, bias=kconst_sb[:, 1:2], scale=-1.0))
    prog.comp("vector", c_v, lambda e: e.tensor_scalar(A_sb[:], A_sb[:], -1.0, None, ALU.mult),
              waits=[(c_s, mA)])
    prog.comp("vector", c_v, lambda e: e.tensor_scalar(omur_sb[:], mur_sb[:], -1.0, 1.0, ALU.mult, ALU.add),
              waits=[(b_consts, m_consts)])
    m_om = prog.comp("vector", c_v, lambda e: e.tensor_scalar(omuk_sb[:], muk_sb[:], -1.0, 1.0, ALU.mult, ALU.add))

    bw = dbatch()
    mw = prog.dma("sync", bw, wstage[:, 0:G * R], wdt1T_d.rearrange("(c p) r -> p c r", p=128))
    mwc = prog.comp("vector", c_v, lambda e: e.tensor_copy(wdt1_bf[:], wstage[:, 0:G * R]), waits=[(bw, mw)])
    bw = dbatch()
    mw = prog.dma("sync", bw, wstage[0:64, :], wdt2T_d[:], waits=[(c_v, mwc)])
    mwc = prog.comp("vector", c_v, lambda e: e.tensor_copy(wdt2_bf[:], wstage[0:64, :]), waits=[(bw, mw)])
    bw = dbatch()
    mw = prog.dma("sync", bw, wstage[:, 0:G * N], wbT_d.rearrange("(c p) n -> p c n", p=128),
                  waits=[(c_v, mwc)])
    mwc = prog.comp("vector", c_v, lambda e: e.tensor_copy(wb_bf[:], wstage[:, 0:G * N]), waits=[(bw, mw)])
    bw = dbatch()
    mw = prog.dma("sync", bw, wstage[:, 0:G * N], wcT_d.rearrange("(c p) n -> p c n", p=128),
                  waits=[(c_v, mwc)])
    prog.comp("vector", c_v, lambda e: e.tensor_copy(wc_bf[:], wstage[:, 0:G * N]),
              waits=[(bw, mw)])
    m_smallw = vsettle()

    m_wprep_sp = [None, None]
    prep_spills = []
    m_wstage_rd = 0
    par = 0
    for wsrc, scls, dsts in [
        (wrT_d, (mur_sb, omur_sb), ("wr1s", "wr2s")),
        (wkT_d, (muk_sb, omuk_sb), ("wk1s", "wk2s")),
        (wvT_d, (None,), ("wvs",)),
    ]:
        for cch in range(G):
            wts = [(c_v, m_smallw), (c_v, m_om)]
            if m_wstage_rd:
                wts.append(m_wstage_rd_edge)
            bld_w = dbatch()
            mld = prog.dma("sync", bld_w, wstage[:], wsrc[cch * 128:(cch + 1) * 128, :], waits=wts)
            for scl, dnm in zip(scls, dsts):
                tile = wprep[:, par * D:(par + 1) * D]
                if scl is None:
                    prog.comp("vector", c_v, lambda e, t=tile: e.tensor_copy(t, wstage[:]),
                              waits=[(bld_w, mld)] + ([m_wprep_sp[par]] if m_wprep_sp[par] else []))
                    mc = vsettle()
                    m_wstage_rd_edge = (c_v, mc)
                else:
                    col = scl[:, cch:cch + 1]
                    prog.comp("scalar", c_s,
                              lambda e, t=tile, col=col: e.activation(t, wstage[:], AF.Copy, bias=0.0, scale=col),
                              waits=[(bld_w, mld)] + ([m_wprep_sp[par]] if m_wprep_sp[par] else []))
                    mc = ssettle()
                    m_wstage_rd_edge = (c_s, mc)
                m_wstage_rd = 1
                bsp = dbatch()
                msp = prog.dma("gpsimd", bsp, wsc[dnm][cch], tile, waits=[m_wstage_rd_edge])
                m_wprep_sp[par] = (bsp, msp)
                prep_spills.append((bsp, msp))
                par ^= 1

    run_ab = "A" in phases
    run_c = "C" in phases
    m_tr_prevlc = 0
    m_evac_all_prev = 0
    m_trT = [0, 0]
    m_evS = [0, 0]
    m_silu = [0, 0, 0]
    m_xcvcast = [0] * G
    m_xcvsp = [None] * G
    m_convfree = [None, None, None]
    m_xfree = [0, 0]
    m_rstd_p = [0, 0]
    m_zfree = 0
    m_bfree = 0
    m_cfree = 0
    m_bsp_b = None
    m_csp_b = None
    m_opst_b = None
    m_rp_used = 0
    m_k_used = 0
    m_v_used = 0
    m_silu_last = 0
    m_kr = 0
    m_ka_g = [0] * G
    m_rsig = 0
    m_ffn_v = 0
    m_ffn_tr = 0
    m_opev = [[0] * G for _ in range(NLT)]
    m_opadd = [0] * NLT
    m_zcast_b = [0, 0]

    def emit_stats(b, lc):
        l0 = lc * LC
        xpar = lc % 2
        xoff = xpar * NLT * D
        moff = xpar * 4 * NLT
        b_x = dbatch()
        for lt in range(NLT):
            prog.dma("sync", b_x, x_ld[:, xoff + lt * D: xoff + (lt + 1) * D],
                     x_d[b, l0 + lt * 128: l0 + (lt + 1) * 128, :],
                     waits=[(c_v, m_xfree[xpar])] if lt == 0 else [])
        m_xld_all = b_x.n
        m_ag = 0
        for lt in range(NLT):
            xt = x_ld[:, xoff + lt * D: xoff + (lt + 1) * D]
            c0 = moff + lt * 4
            prog.comp("vector", c_v, lambda e, xt=xt: e.bn_stats(st6[:, 0:6], xt[:, 0:512]),
                      waits=[(b_x, m_xld_all)])
            mh2 = prog.comp("vector", c_v, lambda e, xt=xt: e.bn_stats(st6[:, 6:12], xt[:, 512:1024]))
            prog.comp("vector", c_v, lambda e, c0=c0: e.bn_aggr(mv[:, c0:c0 + 2], st6[:, 0:6]))
            m_ag = prog.comp("vector", c_v, lambda e, c0=c0: e.bn_aggr(mv[:, c0 + 2:c0 + 4], st6[:, 6:12]),
                             waits=[(c_v, mh2)])
        m1s = mv[:, moff + 0: moff + 4 * NLT: 4]; v1s = mv[:, moff + 1: moff + 4 * NLT: 4]
        m2s = mv[:, moff + 2: moff + 4 * NLT: 4]; v2s = mv[:, moff + 3: moff + 4 * NLT: 4]
        dD = sc2[:, 0:NLT]; dd = sc2[:, NLT:2 * NLT]; sv = sc2[:, 2 * NLT:3 * NLT]
        mm = prog.comp("vector", c_v, lambda e, m1s=m1s, m2s=m2s: e.tensor_tensor(dD, m1s, m2s, ALU.subtract),
                       waits=[(c_v, m_ag)])
        mm = prog.comp("vector", c_v, lambda e, v1s=v1s, v2s=v2s: e.tensor_tensor(sv, v1s, v2s, ALU.add))
        mm2 = prog.comp("vector", c_v, lambda e: e.tensor_tensor(dd, dD, dD, ALU.mult),
                        waits=[(c_v, mm)])
        mm2 = prog.comp("vector", c_v, lambda e: e.tensor_scalar(sv, sv, 0.5, None, ALU.mult))
        mm3 = prog.comp("vector", c_v, lambda e, v1s=v1s: e.scalar_tensor_tensor(v1s, dd, 0.25, sv, ALU.mult, ALU.add),
                        waits=[(c_v, mm2)])
        mm3 = prog.comp("vector", c_v, lambda e, m1s=m1s, m2s=m2s: e.tensor_tensor(m1s, m1s, m2s, ALU.add))
        prog.comp("vector", c_v, lambda e, m1s=m1s: e.tensor_scalar(m1s, m1s, 0.5, None, ALU.mult),
                  waits=[(c_v, mm3)])
        magg = vsettle()
        prog.comp("scalar", c_s,
                  lambda e, xpar=xpar, v1s=v1s: e.activation(
                      rstd[:, xpar * NLT:(xpar + 1) * NLT], v1s,
                      AF.Ln, bias=kconst_sb[:, 2:3], scale=1.0),
                  waits=[(c_v, magg)])
        prog.comp("scalar", c_s,
                  lambda e, xpar=xpar: e.activation(
                      rstd[:, xpar * NLT:(xpar + 1) * NLT], rstd[:, xpar * NLT:(xpar + 1) * NLT],
                      AF.Exp, bias=0.0, scale=kconst_sb[:, 3:4]))
        m_rstd_p[xpar] = ssettle()

    for b in (range(NB) if run_ab else ()):
        for lc in range(NLC):
            l0 = lc * LC
            first = (b == 0 and lc == 0)
            if first:
                emit_stats(0, 0)
            xpar = lc % 2
            xoff = xpar * NLT * D
            m_rstd = m_rstd_p[xpar]

            if lc == 0:
                prog.comp("vector", c_v, lambda e: e.memset(xnw[:, :, 0:4], 0.0),
                          waits=[(c_s, m_evac_all_prev)])
                m_tail = prog.comp("vector", c_v, lambda e: e.memset(xnw_bf[:, :, 0:1], 0.0))
            else:
                prog.comp("vector", c_v, lambda e: e.tensor_copy(xnw[:, :, 0:4], xnw[:, :, LC:LC + 4]),
                          waits=[(c_s, m_evac_all_prev)])
                m_tail = prog.comp("vector", c_v,
                                   lambda e: e.tensor_copy(xnw_bf[:, :, 0:1], xnw_bf[:, :, LC:LC + 1]))
            for lt in range(NLT):
                xt = x_ld[:, xoff + lt * D: xoff + (lt + 1) * D]
                c0 = xpar * 4 * NLT + lt * 4
                m_norm = prog.comp("vector", c_v,
                                   lambda e, xt=xt, lt=lt, c0=c0, xpar=xpar: e.tensor_scalar(
                                       xn_t[:], xt, mv[:, c0:c0 + 1], rstd[:, xpar * NLT + lt: xpar * NLT + lt + 1],
                                       ALU.subtract, ALU.mult),
                                   waits=[(c_s, m_rstd), (c_t, max(m_trT))])
                m_norm = vsettle()
                for half in range(2):
                    bank = pband[half]
                    for j in range(4):
                        g = half * 4 + j
                        m_trT[half] = prog.comp(
                            "tensor", c_t,
                            lambda e, g=g, j=j, bank=bank: e.transpose(
                                bank[:, j * 128:(j + 1) * 128],
                                xn_t[:, g * 128:(g + 1) * 128], ident_sb[:]),
                            waits=[(c_v, m_norm), (c_s, m_evS[half])])
                    for j in range(4):
                        g = half * 4 + j
                        m_evS[half] = prog.comp(
                            "scalar", c_s,
                            lambda e, g=g, j=j, lt=lt, bank=bank: e.activation(
                                xnw[:, g, 4 + lt * 128: 4 + (lt + 1) * 128],
                                bank[:, j * 128:(j + 1) * 128], AF.Copy, bias=0.0, scale=1.0),
                            waits=[(c_t, m_trT[half]), (c_v, m_tail)])
            m_tr_prevlc = max(m_trT)
            m_evac_all_prev = ssettle()

            if "statd" in taps:
                bt = dbatch()
                prog.dma("gpsimd", bt, stat_sc[b, lc, :, 0:4 * NLT], mv[:, 0:4 * NLT],
                         waits=[(c_s, m_rstd)])
                prog.dma("gpsimd", bt, stat_sc[b, lc, :, 4 * NLT:4 * NLT + NLT], rstd[:, 0:NLT])
            if "xnwd" in taps:
                bt = dbatch()
                prog.dma("gpsimd", bt, xnw_sc[b, lc], xnw[:].rearrange("p g c -> p (g c)"),
                         waits=[(c_s, m_evac_all_prev)])
            m_xnbf = prog.comp("vector", c_v,
                               lambda e: e.tensor_copy(xnw_bf[:, :, 1:LC + 1], xnw[:, :, 4:LC + 4]),
                               waits=[(c_s, m_evac_all_prev)])
            m_xnbf = vsettle()

            for g in range(G):
                par = g % 3
                cbank = pband[5 + (g % 3)]
                xgb = xnw_bf
                wz = [(c_s, m_evac_all_prev),
                      (c_v, (m_zfree, m_bfree, m_cfree)[g % 3])]
                if m_convfree[g % 3]:
                    wz.append(m_convfree[g % 3])
                for k in range(4):
                    prog.comp("tensor", c_t,
                              lambda e, g=g, k=k, cbank=cbank: e.matmul(
                                  cbank[:], cdiag_sb[:, (g * 4 + k) * 128:(g * 4 + k + 1) * 128],
                                  xnw[:, g, 1 + k:1 + k + LC],
                                  start=(k == 0), stop=(k == 3), skip_group_check=True),
                              waits=wz if k == 0 else [])
                m_cmmg = c_t.n
                xv = xcv_g[:, par * LC:(par + 1) * LC]
                prog.comp("scalar", c_s,
                          lambda e, xv=xv, g=g, cbank=cbank: e.activation(
                              xv, cbank[:], AF.Silu, bias=cb_sb[:, g:g + 1], scale=1.0),
                          waits=[(c_t, m_cmmg), (c_v, m_xcvcast[(g - 3) % G])]
                          + ([m_xcvsp[(g - 3) % G]] if m_xcvsp[(g - 3) % G] else []))
                m_silu[par] = ssettle()
                m_convfree[g % 3] = (c_s, m_silu[par])
                prog.comp("vector", c_v,
                          lambda e, xv=xv, g=g: e.tensor_copy(
                              xcv_bf[:, g * LC:(g + 1) * LC], xv),
                          waits=[(c_s, m_silu[par])])
                m_xcvcast[g] = vsettle()
                if g >= 1:
                    pv = (g - 1) % 3
                    bsp = dbatch()
                    msp = prog.dma("gpsimd", bsp, xcv_sc[b, g - 1][:, l0:l0 + LC],
                                   xcv_g[:, pv * LC:(pv + 1) * LC],
                                   waits=[(c_s, m_silu[pv])])
                    m_xcvsp[g - 1] = (bsp, msp)
            pv = (G - 1) % 3
            bsp = dbatch()
            msp = prog.dma("gpsimd", bsp, xcv_sc[b, G - 1][:, l0:l0 + LC],
                           xcv_g[:, pv * LC:(pv + 1) * LC], waits=[(c_s, m_silu[pv])])
            m_xcvsp[G - 1] = (bsp, msp)

            nb2, nlc2 = (b, lc + 1) if lc + 1 < NLC else (b + 1, 0)
            if nb2 < NB:
                emit_stats(nb2, nlc2)

            b_rp = dbatch()
            for cch in range(G):
                w0 = [(c_t, m_rp_used)] if cch == 0 else []
                if first and cch == 0:
                    w0 += prep_spills
                prog.dma("sync", b_rp, slotRP[:, cch * 2 * D: cch * 2 * D + D], wsc["wr1s"][cch], waits=w0)
                prog.dma("sync", b_rp, slotRP[:, cch * 2 * D + D: (cch + 1) * 2 * D], wsc["wr2s"][cch])
            m_rp_load = b_rp.n
            b_kv = dbatch()
            for cch in range(G):
                w0 = [(c_t, m_v_used)] if cch == 0 else []
                prog.dma("sync", b_kv, slotKV[:, cch * 2 * D: cch * 2 * D + D], wsc["wk1s"][cch], waits=w0)
                prog.dma("sync", b_kv, slotKV[:, cch * 2 * D + D: (cch + 1) * 2 * D], wsc["wk2s"][cch])
            m_kv_load = b_kv.n

            for dg in range(G):
                for cch in range(G):
                    wz = [(c_v, m_xnbf), (b_kv, m_kv_load)] if cch == 0 else []
                    if cch == 0:
                        wz.append((c_v, m_kr))
                    prog.comp("tensor", c_t,
                              lambda e, dg=dg, cch=cch: e.matmul(
                                  pband[3][:], slotKV[:, cch * 2 * D + dg * 128: cch * 2 * D + (dg + 1) * 128],
                                  xnw_bf[:, cch, 1:LC + 1],
                                  start=(cch == 0), stop=False), waits=wz)
                for cch in range(G):
                    prog.comp("tensor", c_t,
                              lambda e, dg=dg, cch=cch: e.matmul(
                                  pband[3][:], slotKV[:, cch * 2 * D + D + dg * 128: cch * 2 * D + D + (dg + 1) * 128],
                                  xnw_bf[:, cch, 0:LC],
                                  start=False, stop=(cch == G - 1)))
                m_kmm = c_t.n
                prog.comp("vector", c_v,
                          lambda e: e.tensor_scalar(kr_sb[:], pband[3][:], 0.0, None, ALU.max),
                          waits=[(c_t, m_kmm), (c_s, m_ka_g[(dg - 1) % G] if dg >= 1 else 0)])
                m_kr = vsettle()
                prog.comp("scalar", c_s,
                          lambda e, dg=dg: e.activation(
                              ka_bf[:, dg * LC:(dg + 1) * LC], kr_sb[:], AF.Square,
                              bias=0.0, scale=1.0),
                          waits=[(c_v, m_kr)])
                m_ka_g[dg] = ssettle()
            m_k_used = m_kmm

            b_v = dbatch()
            for cch in range(G):
                w0 = [(c_t, m_k_used)] if cch == 0 else []
                prog.dma("sync", b_v, slotKV[:, cch * 2 * D: cch * 2 * D + D], wsc["wvs"][cch], waits=w0)
            m_v_load = b_v.n

            for dg in range(G):
                for cch in range(G):
                    wz = [(c_v, m_xnbf), (b_rp, m_rp_load)] if cch == 0 else []
                    if cch == 0:
                        wz.append((c_s, m_rsig))
                    prog.comp("tensor", c_t,
                              lambda e, dg=dg, cch=cch: e.matmul(
                                  pband[2][:], slotRP[:, cch * 2 * D + dg * 128: cch * 2 * D + (dg + 1) * 128],
                                  xnw_bf[:, cch, 1:LC + 1],
                                  start=(cch == 0), stop=False), waits=wz)
                for cch in range(G):
                    prog.comp("tensor", c_t,
                              lambda e, dg=dg, cch=cch: e.matmul(
                                  pband[2][:], slotRP[:, cch * 2 * D + D + dg * 128: cch * 2 * D + D + (dg + 1) * 128],
                                  xnw_bf[:, cch, 0:LC],
                                  start=False, stop=(cch == G - 1)))
                m_rmm = c_t.n
                m_rp_used = m_rmm
                prog.comp("scalar", c_s,
                          lambda e: e.activation(rtmp[:], pband[2][:], AF.Sigmoid, bias=0.0, scale=1.0),
                          waits=[(c_t, m_rmm), (c_v, m_ffn_v)])
                m_rsig = ssettle()
                for cch in range(G):
                    wz = [(c_s, m_ka_g[cch])]
                    if cch == 0:
                        wz += [(c_v, m_ffn_v), (b_v, m_v_load)]
                    prog.comp("tensor", c_t,
                              lambda e, dg=dg, cch=cch: e.matmul(
                                  pband[4][:], slotKV[:, cch * 2 * D + dg * 128: cch * 2 * D + (dg + 1) * 128],
                                  ka_bf[:, cch * LC:(cch + 1) * LC],
                                  start=(cch == 0), stop=(cch == G - 1)), waits=wz)
                m_vmm = c_t.n
                m_v_used = m_vmm
                m_ffn_v = prog.comp("vector", c_v,
                                    lambda e: e.tensor_tensor(ffn_g[:], rtmp[:], pband[4][:], ALU.mult),
                                    waits=[(c_t, m_vmm), (c_s, m_rsig), (c_t, m_ffn_tr)])
                m_ffn_v = vsettle()
                bank = pband[dg % 2]
                for j in range(NLT):
                    m_ffn_tr = prog.comp("tensor", c_t,
                                         lambda e, j=j, bank=bank: e.transpose(
                                             bank[:, j * 128:(j + 1) * 128],
                                             ffn_g[:, j * 128:(j + 1) * 128], ident_sb[:]),
                                         waits=[(c_v, m_ffn_v), (c_s, m_evS[dg % 2])])
                for j in range(NLT):
                    m_evS[dg % 2] = prog.comp(
                        "scalar", c_s,
                        lambda e, j=j, dg=dg, bank=bank: e.activation(
                            opart_t[:, j * D + dg * 128: j * D + (dg + 1) * 128],
                            bank[:, j * 128:(j + 1) * 128], AF.Copy, bias=0.0, scale=1.0),
                        waits=[(c_t, m_ffn_tr), (c_v, m_opadd[j])]
                        + ([m_opst_b] if m_opst_b else []))
                    m_opev[j][dg] = m_evS[dg % 2]
                if dg == G - 1:
                    mset = ssettle()
                    for j in range(NLT):
                        m_opev[j][dg] = mset

            for cch in range(G):
                wz = [(c_v, m_xcvcast[cch])]
                if cch == 0:
                    wz.append((c_v, m_zfree))
                    if m_convfree[0]:
                        wz.append(m_convfree[0])
                prog.comp("tensor", c_t,
                          lambda e, cch=cch: e.matmul(
                              pband[5][0:64, :], wdt1_bf[:, cch * R:(cch + 1) * R],
                              xcv_bf[:, cch * LC:(cch + 1) * LC],
                              start=(cch == 0), stop=(cch == G - 1)), waits=wz)
            m_zmm = c_t.n
            mze = prog.comp("vector", c_v, lambda e: e.tensor_copy(zev_sb[:], pband[5][0:64, :]),
                            waits=[(c_t, m_zmm)])
            m_zfree = mze
            m_zcast_b[b] = prog.comp("vector", c_v,
                                     lambda e, b=b, l0=l0: e.tensor_copy(
                                         zbf_sb[:, b * L + l0: b * L + l0 + LC], zev_sb[:]))
            m_zcast_b[b] = vsettle()
            for cch in range(G):
                wz = [(c_v, m_xcvcast[cch])]
                if cch == 0:
                    wz.append((c_v, m_bfree))
                    if m_convfree[1]:
                        wz.append(m_convfree[1])
                prog.comp("tensor", c_t,
                          lambda e, cch=cch: e.matmul(
                              pband[6][0:16, :], wb_bf[:, cch * N:(cch + 1) * N],
                              xcv_bf[:, cch * LC:(cch + 1) * LC],
                              start=(cch == 0), stop=(cch == G - 1)), waits=wz)
            m_bmm = c_t.n
            prog.comp("vector", c_v, lambda e: e.tensor_copy(bev_sb[:], pband[6][0:16, :]),
                      waits=[(c_t, m_bmm)] + ([m_bsp_b] if m_bsp_b else []))
            m_bfree = vsettle()
            bsp = dbatch()
            msp = prog.dma("gpsimd", bsp, b_sc[b][:, l0:l0 + LC], bev_sb[:],
                           waits=[(c_v, m_bfree)])
            m_bsp_b = (bsp, msp)
            for cch in range(G):
                wz = [(c_v, m_xcvcast[cch])]
                if cch == 0:
                    wz.append((c_v, m_cfree))
                    if m_convfree[2]:
                        wz.append(m_convfree[2])
                prog.comp("tensor", c_t,
                          lambda e, cch=cch: e.matmul(
                              pband[7][0:16, :], wc_bf[:, cch * N:(cch + 1) * N],
                              xcv_bf[:, cch * LC:(cch + 1) * LC],
                              start=(cch == 0), stop=(cch == G - 1)), waits=wz)
            m_cmm = c_t.n
            prog.comp("vector", c_v, lambda e: e.tensor_copy(cev_sb[:], pband[7][0:16, :]),
                      waits=[(c_t, m_cmm)] + ([m_csp_b] if m_csp_b else []))
            m_cfree = vsettle()
            bsp = dbatch()
            msp = prog.dma("gpsimd", bsp, c_sc[b][:, l0:l0 + LC], cev_sb[:],
                           waits=[(c_v, m_cfree)])
            m_csp_b = (bsp, msp)

            for lt in range(NLT):
                m_opadd[lt] = prog.comp("vector", c_v,
                                        lambda e, lt=lt, xoff=xoff: e.tensor_tensor(
                                            opart_t[:, lt * D:(lt + 1) * D],
                                            opart_t[:, lt * D:(lt + 1) * D],
                                            x_ld[:, xoff + lt * D: xoff + (lt + 1) * D], ALU.add),
                                        waits=[(c_s, m_opev[lt][dgx]) for dgx in range(G)])
                m_opadd[lt] = vsettle()
                m_xfree[xpar] = m_opadd[lt]
                if lt == 0:
                    b_opst = dbatch()
                mop = prog.dma("gpsimd", b_opst, opart_sc[b, l0 + lt * 128: l0 + (lt + 1) * 128, :],
                               opart_t[:, lt * D:(lt + 1) * D], waits=[(c_v, m_opadd[lt])])
                m_opst_b = (b_opst, mop)

    bar = [(c, c.n) for c in ctrs] + [(c, c.n) for c in dpool]
    for eng in ("sync", "vector", "scalar", "tensor", "gpsimd"):
        prog.op(eng, lambda e: e.nop(), waits=[(c, v) for (c, v) in bar if v > 0], inc=None)

    m_bcld = [None, None]
    m_bcmm = [0, 0]
    m_bcev = [0, 0, 0, 0]
    m_bcuse_v = [0, 0]
    m_bcuse_p = [0, 0]
    m_dtmm = [0, 0]
    m_dtfree_v = [0, 0]
    m_dtfree_s = [0, 0]
    m_spl_v = 0
    m_spl_s = 0
    m_dtc = {}
    m_abar = {}
    m_tts = {}
    m_bx = {}
    m_yacc_eng = {}
    m_yeng_g = [None] * G
    m_yclip_eng = [None] * G
    m_carry = {}
    m_hfree = {}
    m_xcvld = 0
    m_cxmark = [0] * G
    m_dxmark = [0] * G
    m_yadd = [0] * G
    m_clip = [0] * G
    m_ytr = [0, 0]
    m_ytr_g = [0] * G
    b_opld_g = [None, None]
    m_ost_b = [None, None]
    m_oadd = [0, 0]
    m_ost = [0, 0]
    m_opldm = [0, 0]
    m_yacc_lastw = [0] * G
    plane = 0

    for b in (range(NB) if run_c else ()):
        for lq in range(NLQ):
            l0 = lq * LQ
            wxc = [(c_s, max(m_dxmark)), (c_v, max(m_cxmark))] if (b + lq) else []
            b_xcv = dbatch()
            for g in range(G):
                m_xcvld = prog.dma("sync", b_xcv, xcv_c[:, g * LQ:(g + 1) * LQ],
                                   xcv_sc[b, g][:, l0:l0 + LQ], waits=wxc if g == 0 else [])
            def load_quarter(q, b=b, l0=l0):
                par = q % 2
                bq = dbatch()
                prog.dma("sync", bq, bcf_sb[0:1, par * 4096: par * 4096 + 2048],
                         b_sc[b][q * 4:(q + 1) * 4, l0:l0 + LQ], waits=[(c_t, m_bcmm[par])])
                mq = prog.dma("sync", bq, bcf_sb[0:1, par * 4096 + 2048: (par + 1) * 4096],
                              c_sc[b][q * 4:(q + 1) * 4, l0:l0 + LQ])
                m_bcld[par] = (bq, mq)
            load_quarter(0)
            load_quarter(1)

            for g in (range(G) if csub >= 1 else ()):
                par = g % 2
                bank = pband[4 + par]
                wz = [(c_v, m_dtfree_v[par]), (c_s, m_dtfree_s[par])]
                prog.comp("tensor", c_t,
                          lambda e, g=g, b=b, l0=l0, bank=bank: e.matmul(
                              bank[:], wdt2_bf[:, g * 128:(g + 1) * 128],
                              zbf_sb[:, b * L + l0: b * L + l0 + LQ],
                              start=True, stop=True), waits=wz)
                m_dtmm[par] = c_t.n
                bcol = bdt_sb[:, g:g + 1]
                mts = prog.comp("vector", c_v,
                                lambda e, bank=bank, bcol=bcol: e.tensor_scalar(
                                    sp3[:], bank[:], bcol, 0.0, ALU.add, ALU.max),
                                waits=[(c_t, m_dtmm[par]), (c_s, m_dtfree_s[par])])
                m_dtfree_v[par] = mts
                prog.comp("scalar", c_s,
                          lambda e, bank=bank, bcol=bcol: e.activation(sp1[:], bank[:], AF.Abs, bias=bcol, scale=1.0),
                          waits=[(c_t, m_dtmm[par]), (c_v, mts), (c_v, m_spl_v)])
                prog.comp("scalar", c_s, lambda e: e.activation(sp2[:], sp1[:], AF.Exp, bias=0.0, scale=-1.0))
                prog.comp("scalar", c_s, lambda e: e.activation(sp1[:], sp2[:], AF.Ln, bias=1.0, scale=1.0))
                m_spl_s = ssettle()
                m_dtfree_s[par] = m_spl_s
                m_spl_v = prog.comp("vector", c_v,
                                    lambda e, g=g: e.tensor_tensor(dt_sb[:, g * LQ:(g + 1) * LQ], sp3[:], sp1[:], ALU.add),
                                    waits=[(c_s, m_spl_s)])
                m_cxmark[g] = prog.comp("vector", c_v,
                                        lambda e, g=g: e.scalar_tensor_tensor(
                                            cx_sb[:, g * LQ:(g + 1) * LQ], dt_sb[:, g * LQ:(g + 1) * LQ],
                                            1.0, xcv_c[:, g * LQ:(g + 1) * LQ], ALU.min, ALU.mult),
                                        waits=[(b_xcv, m_xcvld)])

            def emit_dtc(pl, plane0=None):
                nn, gg = divmod(pl - _pl0[0], G)
                if nn >= N:
                    return
                ppar = pl % 2
                col = gg * N + nn
                prog.comp("vector", c_v,
                          lambda e, gg=gg, col=col, ppar=ppar: e.tensor_scalar(
                              dtc_sb[:, ppar * LQ:(ppar + 1) * LQ],
                              dt_sb[:, gg * LQ:(gg + 1) * LQ],
                              lo_sb[:, col:col + 1], hi_sb[:, col:col + 1],
                              ALU.max, ALU.min),
                          waits=[(c_s, m_abar.get(pl - 2, 0)), (c_v, m_spl_v) if nn == 0 else (c_v, 0)])
                m_dtc[pl] = vsettle()

            def emit_abar(pl, plane0=None):
                nn, gg = divmod(pl - _pl0[0], G)
                if nn >= N:
                    return
                ppar = pl % 2
                col = gg * N + nn
                prog.comp("scalar", c_s,
                          lambda e, col=col, ppar=ppar: e.activation(
                              abar_sb[:, ppar * LQ:(ppar + 1) * LQ],
                              dtc_sb[:, ppar * LQ:(ppar + 1) * LQ],
                              AF.Exp, bias=0.0, scale=A_sb[:, col:col + 1]),
                          waits=[(c_v, m_dtc[pl]), (c_v, m_tts.get(pl - 2, 0))])
                m_abar[pl] = ssettle()

            _pl0 = [plane]
            import os as _os2
            _lead = int(_os2.environ.get("DTLEAD", "2"))
            if _lead:
                emit_dtc(plane)
                emit_abar(plane)
                emit_dtc(plane + 1)
                emit_abar(plane + 1)
            for n in (range(N) if csub >= 2 else ()):
                npar = n % 2
                q = n // 4
                j = n % 4
                qpar = q % 2
                prog.comp("tensor", c_t,
                          lambda e, npar=npar, qpar=qpar, j=j: e.matmul(
                              pband[npar][:], ones_sb[:],
                              bcf_sb[0:1, qpar * 4096 + j * LQ: qpar * 4096 + (j + 1) * LQ],
                              start=True, stop=True),
                          waits=[m_bcld[qpar], (c_s, m_bcev[npar])])
                mbB = c_t.n
                prog.comp("tensor", c_t,
                          lambda e, npar=npar, qpar=qpar, j=j: e.matmul(
                              pband[2 + npar][:], ones_sb[:],
                              bcf_sb[0:1, qpar * 4096 + 2048 + j * LQ: qpar * 4096 + 2048 + (j + 1) * LQ],
                              start=True, stop=True),
                          waits=[(c_s, m_bcev[2 + npar])])
                mbC = c_t.n
                m_bcmm[qpar] = mbC
                m_bcev[npar] = prog.comp("scalar", c_s,
                                         lambda e, npar=npar: e.activation(
                                             Bb_sb[:, npar * LQ:(npar + 1) * LQ], pband[npar][:],
                                             AF.Copy, bias=0.0, scale=1.0),
                                         waits=[(c_t, mbB), (c_v, m_bcuse_v[npar])])
                m_bcev[2 + npar] = prog.comp("scalar", c_s,
                                             lambda e, npar=npar: e.activation(
                                                 Cb_sb[:, npar * LQ:(npar + 1) * LQ], pband[2 + npar][:],
                                                 AF.Copy, bias=0.0, scale=1.0),
                                             waits=[(c_t, mbC), (c_v, m_bcuse_v[npar])])
                m_bcset = ssettle()
                m_bcev_mark = (m_bcset, m_bcset)

                for g in range(G):
                    ppar = plane % 2
                    ccol = b * G * N + g * N + n
                    prog.comp("vector", c_v,
                              lambda e, g=g, npar=npar: e.tensor_tensor(
                                  bx_sb[:, g * LQ:(g + 1) * LQ],
                                  cx_sb[:, g * LQ:(g + 1) * LQ],
                                  Bb_sb[:, npar * LQ:(npar + 1) * LQ], ALU.mult),
                              waits=[(c_s, m_bcev_mark[0])])
                    if not _lead:
                        emit_dtc(plane)
                        emit_abar(plane)
                    init = 0.0 if lq == 0 else hcarry[:, ccol:ccol + 1]
                    m_tts[plane] = prog.comp("vector", c_v,
                                             lambda e, ppar=ppar, init=init, g=g: e.tensor_tensor_scan(
                                                 h_sb[:, ppar * LQ:(ppar + 1) * LQ],
                                                 abar_sb[:, ppar * LQ:(ppar + 1) * LQ],
                                                 bx_sb[:, g * LQ:(g + 1) * LQ],
                                                 init, ALU.mult, ALU.add),
                                             waits=[(c_s, m_abar[plane]),
                                                    (c_p, m_carry.get((b, g, n), 0)),
                                                    (c_p, m_hfree.get(plane - 2, 0))])
                    m_carry[(b, g, n)] = prog.comp("gpsimd", c_p,
                                                   lambda e, ppar=ppar, ccol=ccol: e.tensor_copy(
                                                       hcarry[:, ccol:ccol + 1],
                                                       h_sb[:, ppar * LQ + LQ - 1: ppar * LQ + LQ]),
                                                   waits=[(c_v, m_tts[plane])])
                    m_hfree[plane] = m_carry[(b, g, n)]
                    if _lead:
                        emit_dtc(plane + 2)
                        emit_abar(plane + 2)
                    if n == 0:
                        m_yacc_lastw[g] = prog.comp("vector", c_v,
                                                    lambda e, g=g, ppar=ppar, npar=npar: e.tensor_tensor(
                                                        yacc_sb[:, g * LQ:(g + 1) * LQ],
                                                        h_sb[:, ppar * LQ:(ppar + 1) * LQ],
                                                        Cb_sb[:, npar * LQ:(npar + 1) * LQ], ALU.mult),
                                                    waits=[(c_s, m_bcev_mark[1]), (c_t, m_ytr_g[g])])
                    else:
                        prog.comp("vector", c_v,
                                  lambda e, ppar=ppar, npar=npar: e.tensor_tensor(
                                      hc_sb[:, 0:LQ],
                                      h_sb[:, ppar * LQ:(ppar + 1) * LQ],
                                      Cb_sb[:, npar * LQ:(npar + 1) * LQ], ALU.mult),
                                  waits=[(c_s, m_bcev_mark[1])])
                        m_yacc_lastw[g] = prog.comp("vector", c_v,
                                                    lambda e, g=g: e.tensor_tensor(
                                                        yacc_sb[:, g * LQ:(g + 1) * LQ],
                                                        yacc_sb[:, g * LQ:(g + 1) * LQ],
                                                        hc_sb[:, 0:LQ], ALU.add))
                    m_yeng_g[g] = (c_v, m_yacc_lastw[g])
                    m_carry[(b, g, n)] = prog.comp("gpsimd", c_p,
                                                   lambda e, ppar=ppar, ccol=ccol: e.tensor_copy(
                                                       hcarry[:, ccol:ccol + 1],
                                                       h_sb[:, ppar * LQ + LQ - 1: ppar * LQ + LQ]),
                                                   waits=[(c_v, m_yacc_lastw[g])])
                    m_hfree[plane] = m_carry[(b, g, n)]
                    plane += 1
                m_bcuse_v[npar] = c_v.n
                if n % 4 == 3 and n // 4 + 2 <= 3:
                    load_quarter(n // 4 + 2)

            def emit_out_stores(gg, b=b, l0=l0):
                gp = gg % 2
                bst = dbatch()
                mst = 0
                for lt in range(NLT):
                    w0 = [(c_v, m_oadd[gp])] if lt == 0 else []
                    mst = prog.dma("sync", bst,
                                   out_d[b, l0 + lt * 128: l0 + (lt + 1) * 128, gg * 128:(gg + 1) * 128],
                                   ostg[:, gp * LQ + lt * 128: gp * LQ + (lt + 1) * 128], waits=w0)
                m_ost_b[gp] = (bst, mst)
            for g in (range(G) if csub >= 3 else ()):
                gpar = g % 2
                yeng, yctr = ("vector", c_v)
                b_opld = dbatch()
                for lt in range(NLT):
                    m_opldm[gpar] = prog.dma("sync", b_opld,
                                             opld[:, gpar * LQ + lt * 128: gpar * LQ + (lt + 1) * 128],
                                             opart_sc[b, l0 + lt * 128: l0 + (lt + 1) * 128,
                                                      g * 128:(g + 1) * 128],
                                             waits=[(c_v, m_oadd[gpar])] if lt == 0 else [])
                b_opld_g[gpar] = b_opld
                prog.comp("scalar", c_s,
                          lambda e, g=g: e.activation(
                              dx_sb[:], xcv_c[:, g * LQ:(g + 1) * LQ], AF.Copy,
                              bias=0.0, scale=dcol_sb[:, g:g + 1]),
                          waits=[m_yclip_eng[(g - 1) % G]] if m_yclip_eng[(g - 1) % G] else [])
                m_dxmark[g] = ssettle()

                prog.comp(yeng, yctr,
                          lambda e, g=g: e.tensor_tensor(
                              yacc_sb[:, g * LQ:(g + 1) * LQ],
                              yacc_sb[:, g * LQ:(g + 1) * LQ], dx_sb[:], ALU.add),
                          waits=[(c_s, m_dxmark[g]), m_yeng_g[g]])
                prog.comp(yeng, yctr,
                          lambda e, g=g: e.tensor_scalar(
                              yacc_sb[:, g * LQ:(g + 1) * LQ],
                              yacc_sb[:, g * LQ:(g + 1) * LQ],
                              -50.0, 50.0, ALU.max, ALU.min))
                mclip = vsettle()
                m_yclip_eng[g] = (yctr, mclip)
                bank = pband[6 + gpar]
                for j in range(NLT):
                    m_ytr[gpar] = prog.comp("tensor", c_t,
                                            lambda e, g=g, j=j, bank=bank: e.transpose(
                                                bank[:, j * 128:(j + 1) * 128],
                                                yacc_sb[:, g * LQ + j * 128: g * LQ + (j + 1) * 128],
                                                ident_sb[:]),
                                            waits=[m_yclip_eng[g], (c_v, m_oadd[gpar])])
                m_ytr_g[g] = m_ytr[gpar]
                for j in range(NLT):
                    m_oadd[gpar] = prog.comp("vector", c_v,
                                             lambda e, j=j, gpar=gpar, bank=bank: e.tensor_tensor(
                                                 ostg[:, gpar * LQ + j * 128: gpar * LQ + (j + 1) * 128],
                                                 bank[:, j * 128:(j + 1) * 128],
                                                 opld[:, gpar * LQ + j * 128: gpar * LQ + (j + 1) * 128],
                                                 ALU.add),
                                             waits=[(c_t, m_ytr[gpar]),
                                                    (b_opld_g[gpar], m_opldm[gpar])]
                                             + ([m_ost_b[gpar]] if m_ost_b[gpar] else []))
                m_oadd[gpar] = vsettle()
                if g >= 1:
                    emit_out_stores(g - 1)
            if csub >= 3:
                emit_out_stores(G - 1)

    fin = [(c, c.n) for c in ctrs] + [(c, c.n) for c in dpool]
    for eng in ("gpsimd", "sync"):
        prog.op(eng, lambda e: e.nop(), waits=[(c, v) for (c, v) in fin if v > 0])

    from contextlib import ExitStack
    with ExitStack() as stk:
        for i, c in enumerate(ctrs):
            c.h = stk.enter_context(nc.semaphore(f"sem{i}"))
        for i, c in enumerate(dpool):
            c.h = stk.enter_context(nc.semaphore(f"dsem{i}"))
        blk = stk.enter_context(nc.Block())
        items = prog.items

        def mk(name):
            def emit(eng):
                seen = {}
                for (en, fn, waits, inc) in items:
                    if en != name:
                        continue
                    for (ctr, val) in waits:
                        k = id(ctr)
                        if seen.get(k, 0) < val:
                            eng.wait_ge(ctr.h, val)
                            seen[k] = val
                    inst = fn(eng)
                    if inc is not None:
                        inst.then_inc(inc[0].h, inc[1])
            return emit

        blk.sync(mk("sync"))
        blk.vector(mk("vector"))
        blk.scalar(mk("scalar"))
        blk.tensor(mk("tensor"))
        blk.gpsimd(mk("gpsimd"))

    for (en, fn, waits, inc) in prog.items:
        for (ctr, val) in waits:
            assert val <= ctr.n, f"wait {val} exceeds final {ctr.n} on {en}"
    return nc


def _conv_diag(conv_w):
    cw = conv_w[:, 0, :]
    cd = np.zeros((128, G * 4 * 128), np.float32)
    idx = np.arange(128)
    for g in range(G):
        for k in range(4):
            cd[idx, (g * 4 + k) * 128 + idx] = cw[g * 128:(g + 1) * 128, k]
    return cd


def _layout_inputs(inputs, core):
    i = inputs
    f32 = np.float32

    def percol(v):
        return np.ascontiguousarray(v.reshape(G, 128).T).astype(f32, copy=False)

    m = {
        "x": np.ascontiguousarray(i["L0_attn_x"][2 * core:2 * core + 2]),
        "wrT": np.ascontiguousarray(i["L0_ffn_W_r"].T),
        "wkT": np.ascontiguousarray(i["L0_ffn_W_k"].T),
        "wvT": np.ascontiguousarray(i["L0_ffn_W_v"].T),
        "wdt1T": np.ascontiguousarray(i["L0_attn_W_dt1"].T),
        "wdt2T": np.ascontiguousarray(i["L0_attn_W_dt2"].T),
        "wbT": np.ascontiguousarray(i["L0_attn_W_B"].T),
        "wcT": np.ascontiguousarray(i["L0_attn_W_C"].T),
        "alogp": np.ascontiguousarray(
            i["L0_attn_A_log"].reshape(G, 128, N).transpose(1, 0, 2).reshape(128, G * N)),
        "dcolp": percol(i["L0_attn_D_param"]),
        "bdtp": percol(i["L0_attn_b_dt"]),
        "cbp": percol(i["L0_attn_conv_b"]),
        "murp": percol(i["L0_ffn_mu_r"]),
        "mukp": percol(i["L0_ffn_mu_k"]),
        "convwp": np.ascontiguousarray(
            i["L0_attn_conv_w"][:, 0, :].reshape(G, 128, 4).transpose(1, 0, 2).reshape(128, G * 4)),
        "convdg": _conv_diag(i["L0_attn_conv_w"]),
        "onesr": np.ones((1, 128), f32),
        "kconst": np.tile(np.array([[C_LO, C_HI, EPS, -0.5]], f32), (128, 1)),
        "identm": np.eye(128, dtype=f32),
    }
    return m


_nc_cache = {}


def kernel(**inputs):
    key = "default"
    if key not in _nc_cache:
        _nc_cache[key] = build_program(taps=_TAPS)
    nc = _nc_cache[key]
    in_maps = [_layout_inputs(inputs, core) for core in range(NCORES)]
    res = run_bass_kernel_spmd(nc, in_maps, core_ids=list(range(NCORES)))
    out = np.concatenate([res.results[c]["out"] for c in range(NCORES)], axis=0)
    return out


# version 18 (final)
# speedup vs baseline: 1.0021x; 1.0021x over previous
import math
import numpy as np

import concourse.bass as bass
import concourse.mybir as mybir
from concourse.bass_utils import run_bass_kernel_spmd

F32 = mybir.dt.float32
BF16 = mybir.dt.bfloat16
AF = mybir.ActivationFunctionType
ALU = mybir.AluOpType

NCORES = 8
NB = 2
L = 2048
D = 1024
G = 8
N = 16
R = 64
LC = 512
NLC = L // LC
NLT = LC // 128
LQ = 512
NLQ = L // LQ

C_LO = math.log(-math.log(0.999))
C_HI = math.log(10.0)
EPS = 1e-5

_TAPS = ()
_PHASES = "PAC"


class Ctr:
    __slots__ = ("h", "n")

    def __init__(self):
        self.h = None
        self.n = 0


class Prog:
    def __init__(self):
        self.items = []

    def op(self, eng, fn, waits=(), inc=None):
        ws = []
        for (c, v) in waits:
            if v > 0:
                assert v <= c.n + 10**9
                ws.append((c, v))
        self.items.append((eng, fn, ws, inc))

    def dma(self, eng, ctr, out, in_, waits=()):
        def f(e, out=out, in_=in_):
            return e.dma_start(out=out, in_=in_)
        ctr.n += 16
        self.op(eng, f, waits, (ctr, 16))
        return ctr.n

    def comp(self, eng, ctr, fn, waits=()):
        ctr.n += 1
        self.op(eng, fn, waits, (ctr, 1))
        return ctr.n


class SBAlloc:
    def __init__(self, nc, base=0):
        self.nc = nc
        self.off = base

    def alloc(self, name, shape, dtype):
        nbytes = int(np.prod(shape[1:])) * mybir.dt.size(dtype)
        off = (self.off + 31) & ~31
        t = self.nc.alloc_sbuf_tensor_at(name, list(shape), dtype, offset=off)
        self.off = off + nbytes
        return t


def build_program(taps=(), phases=None):
    phases = phases or _PHASES
    csub = 3
    if phases and phases[-1].isdigit():
        csub = int(phases[-1])
        phases = phases[:-1]
    nc = bass.Bass("TRN2", target_bir_lowering=False, debug=False)

    def kind(nm):
        return "ExternalOutput" if nm in taps else "Internal"

    x_d = nc.dram_tensor("x", [NB, L, D], F32, kind="ExternalInput")
    out_d = nc.dram_tensor("out", [NB, L, D], F32, kind="ExternalOutput")
    wrT_d = nc.dram_tensor("wrT", [D, D], F32, kind="ExternalInput")
    wkT_d = nc.dram_tensor("wkT", [D, D], F32, kind="ExternalInput")
    wvT_d = nc.dram_tensor("wvT", [D, D], F32, kind="ExternalInput")
    wdt1T_d = nc.dram_tensor("wdt1T", [D, R], F32, kind="ExternalInput")
    wdt2T_d = nc.dram_tensor("wdt2T", [R, D], F32, kind="ExternalInput")
    wbT_d = nc.dram_tensor("wbT", [D, N], F32, kind="ExternalInput")
    wcT_d = nc.dram_tensor("wcT", [D, N], F32, kind="ExternalInput")
    alog_d = nc.dram_tensor("alogp", [128, G * N], F32, kind="ExternalInput")
    dcol_d = nc.dram_tensor("dcolp", [128, G], F32, kind="ExternalInput")
    bdt_d = nc.dram_tensor("bdtp", [128, G], F32, kind="ExternalInput")
    cb_d = nc.dram_tensor("cbp", [128, G], F32, kind="ExternalInput")
    mur_d = nc.dram_tensor("murp", [128, G], F32, kind="ExternalInput")
    muk_d = nc.dram_tensor("mukp", [128, G], F32, kind="ExternalInput")
    convw_d = nc.dram_tensor("convwp", [128, G * 4], F32, kind="ExternalInput")
    convdg_d = nc.dram_tensor("convdg", [128, G * 4 * 128], F32, kind="ExternalInput")
    ones_d = nc.dram_tensor("onesr", [1, 128], F32, kind="ExternalInput")
    kconst_d = nc.dram_tensor("kconst", [128, 4], F32, kind="ExternalInput")
    ident_d = nc.dram_tensor("identm", [128, 128], F32, kind="ExternalInput")

    xcv_sc = nc.dram_tensor("xcv", [NB, G, 128, L], F32, kind=kind("xcv"))
    opart_sc = nc.dram_tensor("opart", [NB, L, D], F32, kind=kind("opart"))
    b_sc = nc.dram_tensor("bsc", [NB, N, L], F32, kind=kind("bsc"))
    c_sc = nc.dram_tensor("csc", [NB, N, L], F32, kind=kind("csc"))
    wsc = {nm: nc.dram_tensor(nm, [G, 128, D], BF16, kind="Internal")
           for nm in ("wr1s", "wr2s", "wk1s", "wk2s", "wvs")}
    xnw_sc = nc.dram_tensor("xnwd", [NB, NLC, 128, G * (LC + 4)], F32, kind=kind("xnwd"))
    stat_sc = nc.dram_tensor("statd", [NB, NLC, 128, 5 * NLT], F32, kind=kind("statd"))

    base0 = (nc.SBUF_PARTITION_SIZE_BYTES - nc.sbuf_bytes_remaining + 31) & ~31
    al = SBAlloc(nc, base0)
    ident_sb = al.alloc("identsb", [128, 128], F32)
    kconst_sb = al.alloc("kconstsb", [128, 4], F32)
    ones_sb = al.alloc("onessb", [1, 128], F32)
    alog_sb = al.alloc("alogsb", [128, G * N], F32)
    A_sb = al.alloc("Asb", [128, G * N], F32)
    lo_sb = al.alloc("losb", [128, G * N], F32)
    hi_sb = al.alloc("hisb", [128, G * N], F32)
    dcol_sb = al.alloc("dcolsb", [128, G], F32)
    bdt_sb = al.alloc("bdtsb", [128, G], F32)
    cb_sb = al.alloc("cbsb", [128, G], F32)
    mur_sb = al.alloc("mursb", [128, G], F32)
    muk_sb = al.alloc("muksb", [128, G], F32)
    omur_sb = al.alloc("omursb", [128, G], F32)
    omuk_sb = al.alloc("omuksb", [128, G], F32)
    convw_sb = al.alloc("convwsb", [128, G * 4], F32)
    cdiag_sb = al.alloc("cdiagsb", [128, G * 4 * 128], F32)
    wdt1_bf = al.alloc("wdt1bf", [128, G * R], BF16)
    wdt2_bf = al.alloc("wdt2bf", [64, D], BF16)
    wb_bf = al.alloc("wbbf", [128, G * N], BF16)
    wc_bf = al.alloc("wcbf", [128, G * N], BF16)
    zbf_sb = al.alloc("zbfsb", [64, NB * L], BF16)
    hcarry = al.alloc("hcarry", [128, NB * G * N], F32)
    vset_sb = al.alloc("vsetsb", [128, 32], F32)
    sset_sb = al.alloc("ssetsb", [128, 32], F32)
    pbase = al.off

    ab = SBAlloc(nc, pbase)
    slotRP = ab.alloc("slotRP", [128, 2 * G * D], BF16)
    slotKV = ab.alloc("slotKV", [128, 2 * G * D], BF16)
    xnw = ab.alloc("xnw", [128, G, LC + 4], F32)
    xnw_bf = ab.alloc("xnwbf", [128, G, LC + 1], BF16)
    x_ld = ab.alloc("xld", [128, 2 * NLT * D], F32)
    xn_t = ab.alloc("xnt", [128, D], F32)
    st6 = ab.alloc("st6", [128, 12], F32)
    mv = ab.alloc("mv", [128, 2 * 4 * NLT], F32)
    sc2 = ab.alloc("sc2", [128, 3 * NLT], F32)
    rstd = ab.alloc("rstd", [128, 2 * NLT], F32)
    xcv_g = ab.alloc("xcvg", [128, 3 * LC], F32)
    xcv_bf = ab.alloc("xcvbf", [128, G * LC], BF16)
    rtmp = ab.alloc("rtmp", [128, LC], F32)
    kr_sb = ab.alloc("krsb", [128, LC], F32)
    ka_bf = ab.alloc("kabf", [128, G * LC], BF16)
    ffn_g = ab.alloc("ffng", [128, LC], F32)
    opart_t = ab.alloc("opartt", [128, NLT * D], F32)
    bev_sb = ab.alloc("bevsb", [16, LC], F32)
    cev_sb = ab.alloc("cevsb", [16, LC], F32)
    zev_sb = ab.alloc("zevsb", [64, LC], F32)
    ab_end = ab.off

    cc = SBAlloc(nc, pbase)
    dt_sb = cc.alloc("dtsb", [128, G * LQ], F32)
    cx_sb = cc.alloc("cxsb", [128, G * LQ], F32)
    yacc_sb = cc.alloc("yaccsb", [128, G * LQ], F32)
    xcv_c = cc.alloc("xcvc", [128, G * LQ], F32)
    bcf_sb = cc.alloc("bcfsb", [1, 2 * 2 * 4 * LQ], F32)
    Bb_sb = cc.alloc("Bbsb", [128, 2 * LQ], F32)
    Cb_sb = cc.alloc("Cbsb", [128, 2 * LQ], F32)
    h_sb = cc.alloc("hsb", [128, 2 * LQ], F32)
    dtc_sb = cc.alloc("dtcsb", [128, 2 * LQ], F32)
    abar_sb = cc.alloc("abarsb", [128, 2 * LQ], F32)
    hc_sb = cc.alloc("hcsb", [128, 2 * LQ], F32)
    bx_sb = cc.alloc("bxsb", [128, G * LQ], F32)
    sp1 = cc.alloc("sp1", [128, LQ], F32)
    sp2 = cc.alloc("sp2", [128, LQ], F32)
    sp3 = cc.alloc("sp3", [128, LQ], F32)
    dx_sb = cc.alloc("dxsb", [128, LQ], F32)
    ostg = cc.alloc("ostg", [128, 2 * LQ], F32)
    opld = cc.alloc("opld", [128, 2 * LQ], F32)
    c_end = cc.off

    wp = SBAlloc(nc, pbase)
    wstage = wp.alloc("wstage", [128, D], F32)
    wprep = wp.alloc("wprep", [128, 2 * D], BF16)
    total = max(ab_end, c_end, wp.off)
    assert total <= nc.SBUF_PARTITION_SIZE_BYTES, f"SBUF over budget: {total}"

    pband = [nc.alloc_psum_tensor(f"pband{i}", [128, 512], F32) for i in range(8)]

    prog = Prog()
    c_ld = Ctr()
    c_st = Ctr()
    c_v = Ctr()
    c_s = Ctr()
    c_t = Ctr()
    c_p = Ctr()
    ctrs = [c_ld, c_st, c_v, c_s, c_t, c_p]
    dpool = [Ctr() for _ in range(32)]
    _dpi = [0]

    def dbatch():
        c = dpool[_dpi[0] % len(dpool)]
        _dpi[0] += 1
        return c

    def vsettle():
        return prog.comp("vector", c_v, lambda e: e.memset(vset_sb[:], 0.0))

    def ssettle():
        return prog.comp("scalar", c_s, lambda e: e.memzero(sset_sb[:, 0:1]))

    prog.comp("scalar", c_s, lambda e: e.memzero(sset_sb[:]))
    b_consts = dbatch()
    for dst, src in [(ident_sb[:], ident_d[:]), (ones_sb[:], ones_d[:]),
                     (alog_sb[:], alog_d[:]), (dcol_sb[:], dcol_d[:]),
                     (bdt_sb[:], bdt_d[:]), (cb_sb[:], cb_d[:]),
                     (mur_sb[:], mur_d[:]), (muk_sb[:], muk_d[:]),
                     (convw_sb[:], convw_d[:]), (kconst_sb[:], kconst_d[:]),
                     (cdiag_sb[:], convdg_d[:])]:
        prog.dma("sync", b_consts, dst, src)
    m_consts = b_consts.n
    mA = prog.comp("scalar", c_s, lambda e: e.activation(A_sb[:], alog_sb[:], AF.Exp, bias=0.0, scale=1.0),
                   waits=[(b_consts, m_consts)])
    prog.comp("scalar", c_s, lambda e: e.activation(lo_sb[:], alog_sb[:], AF.Exp, bias=kconst_sb[:, 0:1], scale=-1.0))
    prog.comp("scalar", c_s, lambda e: e.activation(hi_sb[:], alog_sb[:], AF.Exp, bias=kconst_sb[:, 1:2], scale=-1.0))
    prog.comp("vector", c_v, lambda e: e.tensor_scalar(A_sb[:], A_sb[:], -1.0, None, ALU.mult),
              waits=[(c_s, mA)])
    prog.comp("vector", c_v, lambda e: e.tensor_scalar(omur_sb[:], mur_sb[:], -1.0, 1.0, ALU.mult, ALU.add),
              waits=[(b_consts, m_consts)])
    m_om = prog.comp("vector", c_v, lambda e: e.tensor_scalar(omuk_sb[:], muk_sb[:], -1.0, 1.0, ALU.mult, ALU.add))

    bw = dbatch()
    mw = prog.dma("sync", bw, wstage[:, 0:G * R], wdt1T_d.rearrange("(c p) r -> p c r", p=128))
    mwc = prog.comp("vector", c_v, lambda e: e.tensor_copy(wdt1_bf[:], wstage[:, 0:G * R]), waits=[(bw, mw)])
    bw = dbatch()
    mw = prog.dma("sync", bw, wstage[0:64, :], wdt2T_d[:], waits=[(c_v, mwc)])
    mwc = prog.comp("vector", c_v, lambda e: e.tensor_copy(wdt2_bf[:], wstage[0:64, :]), waits=[(bw, mw)])
    bw = dbatch()
    mw = prog.dma("sync", bw, wstage[:, 0:G * N], wbT_d.rearrange("(c p) n -> p c n", p=128),
                  waits=[(c_v, mwc)])
    mwc = prog.comp("vector", c_v, lambda e: e.tensor_copy(wb_bf[:], wstage[:, 0:G * N]), waits=[(bw, mw)])
    bw = dbatch()
    mw = prog.dma("sync", bw, wstage[:, 0:G * N], wcT_d.rearrange("(c p) n -> p c n", p=128),
                  waits=[(c_v, mwc)])
    prog.comp("vector", c_v, lambda e: e.tensor_copy(wc_bf[:], wstage[:, 0:G * N]),
              waits=[(bw, mw)])
    m_smallw = vsettle()

    m_wprep_sp = [None, None]
    prep_spills = []
    m_wstage_rd = 0
    par = 0
    for wsrc, scls, dsts in [
        (wrT_d, (mur_sb, omur_sb), ("wr1s", "wr2s")),
        (wkT_d, (muk_sb, omuk_sb), ("wk1s", "wk2s")),
        (wvT_d, (None,), ("wvs",)),
    ]:
        for cch in range(G):
            wts = [(c_v, m_smallw), (c_v, m_om)]
            if m_wstage_rd:
                wts.append(m_wstage_rd_edge)
            bld_w = dbatch()
            mld = prog.dma("sync", bld_w, wstage[:], wsrc[cch * 128:(cch + 1) * 128, :], waits=wts)
            for scl, dnm in zip(scls, dsts):
                tile = wprep[:, par * D:(par + 1) * D]
                if scl is None:
                    prog.comp("vector", c_v, lambda e, t=tile: e.tensor_copy(t, wstage[:]),
                              waits=[(bld_w, mld)] + ([m_wprep_sp[par]] if m_wprep_sp[par] else []))
                    mc = vsettle()
                    m_wstage_rd_edge = (c_v, mc)
                else:
                    col = scl[:, cch:cch + 1]
                    prog.comp("scalar", c_s,
                              lambda e, t=tile, col=col: e.activation(t, wstage[:], AF.Copy, bias=0.0, scale=col),
                              waits=[(bld_w, mld)] + ([m_wprep_sp[par]] if m_wprep_sp[par] else []))
                    mc = ssettle()
                    m_wstage_rd_edge = (c_s, mc)
                m_wstage_rd = 1
                bsp = dbatch()
                msp = prog.dma("gpsimd", bsp, wsc[dnm][cch], tile, waits=[m_wstage_rd_edge])
                m_wprep_sp[par] = (bsp, msp)
                prep_spills.append((bsp, msp))
                par ^= 1

    run_ab = "A" in phases
    run_c = "C" in phases
    m_tr_prevlc = 0
    m_evac_all_prev = 0
    m_trT = [0, 0]
    m_evS = [0, 0]
    m_silu = [0, 0, 0]
    m_xcvcast = [0] * G
    m_xcvsp = [None] * G
    m_convfree = [None, None, None]
    m_xfree = [0, 0]
    m_rstd_p = [0, 0]
    m_zfree = 0
    m_bfree = 0
    m_cfree = 0
    m_bsp_b = None
    m_csp_b = None
    m_opst_b = None
    m_rp_used = 0
    m_k_used = 0
    m_v_used = 0
    m_silu_last = 0
    m_kr = 0
    m_ka_g = [0] * G
    m_rsig = 0
    m_ffn_v = 0
    m_ffn_tr = 0
    m_opev = [[0] * G for _ in range(NLT)]
    m_opadd = [0] * NLT
    m_zcast_b = [0, 0]

    def emit_stats(b, lc):
        l0 = lc * LC
        xpar = lc % 2
        xoff = xpar * NLT * D
        moff = xpar * 4 * NLT
        b_x = dbatch()
        for lt in range(NLT):
            prog.dma("sync", b_x, x_ld[:, xoff + lt * D: xoff + (lt + 1) * D],
                     x_d[b, l0 + lt * 128: l0 + (lt + 1) * 128, :],
                     waits=[(c_v, m_xfree[xpar])] if lt == 0 else [])
        m_xld_all = b_x.n
        m_ag = 0
        for lt in range(NLT):
            xt = x_ld[:, xoff + lt * D: xoff + (lt + 1) * D]
            c0 = moff + lt * 4
            prog.comp("vector", c_v, lambda e, xt=xt: e.bn_stats(st6[:, 0:6], xt[:, 0:512]),
                      waits=[(b_x, m_xld_all)])
            mh2 = prog.comp("vector", c_v, lambda e, xt=xt: e.bn_stats(st6[:, 6:12], xt[:, 512:1024]))
            prog.comp("vector", c_v, lambda e, c0=c0: e.bn_aggr(mv[:, c0:c0 + 2], st6[:, 0:6]))
            m_ag = prog.comp("vector", c_v, lambda e, c0=c0: e.bn_aggr(mv[:, c0 + 2:c0 + 4], st6[:, 6:12]),
                             waits=[(c_v, mh2)])
        m1s = mv[:, moff + 0: moff + 4 * NLT: 4]; v1s = mv[:, moff + 1: moff + 4 * NLT: 4]
        m2s = mv[:, moff + 2: moff + 4 * NLT: 4]; v2s = mv[:, moff + 3: moff + 4 * NLT: 4]
        dD = sc2[:, 0:NLT]; dd = sc2[:, NLT:2 * NLT]; sv = sc2[:, 2 * NLT:3 * NLT]
        mm = prog.comp("vector", c_v, lambda e, m1s=m1s, m2s=m2s: e.tensor_tensor(dD, m1s, m2s, ALU.subtract),
                       waits=[(c_v, m_ag)])
        mm = prog.comp("vector", c_v, lambda e, v1s=v1s, v2s=v2s: e.tensor_tensor(sv, v1s, v2s, ALU.add))
        mm2 = prog.comp("vector", c_v, lambda e: e.tensor_tensor(dd, dD, dD, ALU.mult),
                        waits=[(c_v, mm)])
        mm2 = prog.comp("vector", c_v, lambda e: e.tensor_scalar(sv, sv, 0.5, None, ALU.mult))
        mm3 = prog.comp("vector", c_v, lambda e, v1s=v1s: e.scalar_tensor_tensor(v1s, dd, 0.25, sv, ALU.mult, ALU.add),
                        waits=[(c_v, mm2)])
        mm3 = prog.comp("vector", c_v, lambda e, m1s=m1s, m2s=m2s: e.tensor_tensor(m1s, m1s, m2s, ALU.add))
        prog.comp("vector", c_v, lambda e, m1s=m1s: e.tensor_scalar(m1s, m1s, 0.5, None, ALU.mult),
                  waits=[(c_v, mm3)])
        magg = vsettle()
        prog.comp("scalar", c_s,
                  lambda e, xpar=xpar, v1s=v1s: e.activation(
                      rstd[:, xpar * NLT:(xpar + 1) * NLT], v1s,
                      AF.Ln, bias=kconst_sb[:, 2:3], scale=1.0),
                  waits=[(c_v, magg)])
        prog.comp("scalar", c_s,
                  lambda e, xpar=xpar: e.activation(
                      rstd[:, xpar * NLT:(xpar + 1) * NLT], rstd[:, xpar * NLT:(xpar + 1) * NLT],
                      AF.Exp, bias=0.0, scale=kconst_sb[:, 3:4]))
        m_rstd_p[xpar] = ssettle()

    for b in (range(NB) if run_ab else ()):
        for lc in range(NLC):
            l0 = lc * LC
            first = (b == 0 and lc == 0)
            if first:
                emit_stats(0, 0)
            xpar = lc % 2
            xoff = xpar * NLT * D
            m_rstd = m_rstd_p[xpar]

            if lc == 0:
                prog.comp("vector", c_v, lambda e: e.memset(xnw[:, :, 0:4], 0.0),
                          waits=[(c_s, m_evac_all_prev)])
                m_tail = prog.comp("vector", c_v, lambda e: e.memset(xnw_bf[:, :, 0:1], 0.0))
            else:
                prog.comp("vector", c_v, lambda e: e.tensor_copy(xnw[:, :, 0:4], xnw[:, :, LC:LC + 4]),
                          waits=[(c_s, m_evac_all_prev)])
                m_tail = prog.comp("vector", c_v,
                                   lambda e: e.tensor_copy(xnw_bf[:, :, 0:1], xnw_bf[:, :, LC:LC + 1]))
            for lt in range(NLT):
                xt = x_ld[:, xoff + lt * D: xoff + (lt + 1) * D]
                c0 = xpar * 4 * NLT + lt * 4
                m_norm = prog.comp("vector", c_v,
                                   lambda e, xt=xt, lt=lt, c0=c0, xpar=xpar: e.tensor_scalar(
                                       xn_t[:], xt, mv[:, c0:c0 + 1], rstd[:, xpar * NLT + lt: xpar * NLT + lt + 1],
                                       ALU.subtract, ALU.mult),
                                   waits=[(c_s, m_rstd), (c_t, max(m_trT))])
                m_norm = vsettle()
                for half in range(2):
                    bank = pband[half]
                    for j in range(4):
                        g = half * 4 + j
                        m_trT[half] = prog.comp(
                            "tensor", c_t,
                            lambda e, g=g, j=j, bank=bank: e.transpose(
                                bank[:, j * 128:(j + 1) * 128],
                                xn_t[:, g * 128:(g + 1) * 128], ident_sb[:]),
                            waits=[(c_v, m_norm), (c_s, m_evS[half])])
                    for j in range(4):
                        g = half * 4 + j
                        m_evS[half] = prog.comp(
                            "scalar", c_s,
                            lambda e, g=g, j=j, lt=lt, bank=bank: e.activation(
                                xnw[:, g, 4 + lt * 128: 4 + (lt + 1) * 128],
                                bank[:, j * 128:(j + 1) * 128], AF.Copy, bias=0.0, scale=1.0),
                            waits=[(c_t, m_trT[half]), (c_v, m_tail)])
            m_tr_prevlc = max(m_trT)
            m_evac_all_prev = ssettle()

            if "statd" in taps:
                bt = dbatch()
                prog.dma("gpsimd", bt, stat_sc[b, lc, :, 0:4 * NLT], mv[:, 0:4 * NLT],
                         waits=[(c_s, m_rstd)])
                prog.dma("gpsimd", bt, stat_sc[b, lc, :, 4 * NLT:4 * NLT + NLT], rstd[:, 0:NLT])
            if "xnwd" in taps:
                bt = dbatch()
                prog.dma("gpsimd", bt, xnw_sc[b, lc], xnw[:].rearrange("p g c -> p (g c)"),
                         waits=[(c_s, m_evac_all_prev)])
            m_xnbf = prog.comp("vector", c_v,
                               lambda e: e.tensor_copy(xnw_bf[:, :, 1:LC + 1], xnw[:, :, 4:LC + 4]),
                               waits=[(c_s, m_evac_all_prev)])
            m_xnbf = vsettle()

            for g in range(G):
                par = g % 3
                cbank = pband[5 + (g % 3)]
                xgb = xnw_bf
                wz = [(c_s, m_evac_all_prev),
                      (c_v, (m_zfree, m_bfree, m_cfree)[g % 3])]
                if m_convfree[g % 3]:
                    wz.append(m_convfree[g % 3])
                for k in range(4):
                    prog.comp("tensor", c_t,
                              lambda e, g=g, k=k, cbank=cbank: e.matmul(
                                  cbank[:], cdiag_sb[:, (g * 4 + k) * 128:(g * 4 + k + 1) * 128],
                                  xnw[:, g, 1 + k:1 + k + LC],
                                  start=(k == 0), stop=(k == 3), skip_group_check=True),
                              waits=wz if k == 0 else [])
                m_cmmg = c_t.n
                xv = xcv_g[:, par * LC:(par + 1) * LC]
                prog.comp("scalar", c_s,
                          lambda e, xv=xv, g=g, cbank=cbank: e.activation(
                              xv, cbank[:], AF.Silu, bias=cb_sb[:, g:g + 1], scale=1.0),
                          waits=[(c_t, m_cmmg), (c_v, m_xcvcast[(g - 3) % G])]
                          + ([m_xcvsp[(g - 3) % G]] if m_xcvsp[(g - 3) % G] else []))
                m_silu[par] = ssettle()
                m_convfree[g % 3] = (c_s, m_silu[par])
                prog.comp("vector", c_v,
                          lambda e, xv=xv, g=g: e.tensor_copy(
                              xcv_bf[:, g * LC:(g + 1) * LC], xv),
                          waits=[(c_s, m_silu[par])])
                m_xcvcast[g] = vsettle()
                if g >= 1:
                    pv = (g - 1) % 3
                    bsp = dbatch()
                    msp = prog.dma("gpsimd", bsp, xcv_sc[b, g - 1][:, l0:l0 + LC],
                                   xcv_g[:, pv * LC:(pv + 1) * LC],
                                   waits=[(c_s, m_silu[pv])])
                    m_xcvsp[g - 1] = (bsp, msp)
            pv = (G - 1) % 3
            bsp = dbatch()
            msp = prog.dma("gpsimd", bsp, xcv_sc[b, G - 1][:, l0:l0 + LC],
                           xcv_g[:, pv * LC:(pv + 1) * LC], waits=[(c_s, m_silu[pv])])
            m_xcvsp[G - 1] = (bsp, msp)

            nb2, nlc2 = (b, lc + 1) if lc + 1 < NLC else (b + 1, 0)
            if nb2 < NB:
                emit_stats(nb2, nlc2)

            b_rp = dbatch()
            for cch in range(G):
                w0 = [(c_t, m_rp_used)] if cch == 0 else []
                if first and cch == 0:
                    w0 += prep_spills
                prog.dma("sync", b_rp, slotRP[:, cch * 2 * D: cch * 2 * D + D], wsc["wr1s"][cch], waits=w0)
                prog.dma("sync", b_rp, slotRP[:, cch * 2 * D + D: (cch + 1) * 2 * D], wsc["wr2s"][cch])
            m_rp_load = b_rp.n
            b_kv = dbatch()
            for cch in range(G):
                w0 = [(c_t, m_v_used)] if cch == 0 else []
                prog.dma("sync", b_kv, slotKV[:, cch * 2 * D: cch * 2 * D + D], wsc["wk1s"][cch], waits=w0)
                prog.dma("sync", b_kv, slotKV[:, cch * 2 * D + D: (cch + 1) * 2 * D], wsc["wk2s"][cch])
            m_kv_load = b_kv.n

            for dg in range(G):
                for cch in range(G):
                    wz = [(c_v, m_xnbf), (b_kv, m_kv_load)] if cch == 0 else []
                    if cch == 0:
                        wz.append((c_v, m_kr))
                    prog.comp("tensor", c_t,
                              lambda e, dg=dg, cch=cch: e.matmul(
                                  pband[3][:], slotKV[:, cch * 2 * D + dg * 128: cch * 2 * D + (dg + 1) * 128],
                                  xnw_bf[:, cch, 1:LC + 1],
                                  start=(cch == 0), stop=False), waits=wz)
                for cch in range(G):
                    prog.comp("tensor", c_t,
                              lambda e, dg=dg, cch=cch: e.matmul(
                                  pband[3][:], slotKV[:, cch * 2 * D + D + dg * 128: cch * 2 * D + D + (dg + 1) * 128],
                                  xnw_bf[:, cch, 0:LC],
                                  start=False, stop=(cch == G - 1)))
                m_kmm = c_t.n
                prog.comp("vector", c_v,
                          lambda e: e.tensor_scalar(kr_sb[:], pband[3][:], 0.0, None, ALU.max),
                          waits=[(c_t, m_kmm), (c_s, m_ka_g[(dg - 1) % G] if dg >= 1 else 0)])
                m_kr = vsettle()
                prog.comp("scalar", c_s,
                          lambda e, dg=dg: e.activation(
                              ka_bf[:, dg * LC:(dg + 1) * LC], kr_sb[:], AF.Square,
                              bias=0.0, scale=1.0),
                          waits=[(c_v, m_kr)])
                m_ka_g[dg] = ssettle()
            m_k_used = m_kmm

            b_v = dbatch()
            for cch in range(G):
                w0 = [(c_t, m_k_used)] if cch == 0 else []
                prog.dma("sync", b_v, slotKV[:, cch * 2 * D: cch * 2 * D + D], wsc["wvs"][cch], waits=w0)
            m_v_load = b_v.n

            for dg in range(G):
                for cch in range(G):
                    wz = [(c_v, m_xnbf), (b_rp, m_rp_load)] if cch == 0 else []
                    if cch == 0:
                        wz.append((c_s, m_rsig))
                    prog.comp("tensor", c_t,
                              lambda e, dg=dg, cch=cch: e.matmul(
                                  pband[2][:], slotRP[:, cch * 2 * D + dg * 128: cch * 2 * D + (dg + 1) * 128],
                                  xnw_bf[:, cch, 1:LC + 1],
                                  start=(cch == 0), stop=False), waits=wz)
                for cch in range(G):
                    prog.comp("tensor", c_t,
                              lambda e, dg=dg, cch=cch: e.matmul(
                                  pband[2][:], slotRP[:, cch * 2 * D + D + dg * 128: cch * 2 * D + D + (dg + 1) * 128],
                                  xnw_bf[:, cch, 0:LC],
                                  start=False, stop=(cch == G - 1)))
                m_rmm = c_t.n
                m_rp_used = m_rmm
                prog.comp("scalar", c_s,
                          lambda e: e.activation(rtmp[:], pband[2][:], AF.Sigmoid, bias=0.0, scale=1.0),
                          waits=[(c_t, m_rmm), (c_v, m_ffn_v)])
                m_rsig = ssettle()
                for cch in range(G):
                    wz = [(c_s, m_ka_g[cch])]
                    if cch == 0:
                        wz += [(c_v, m_ffn_v), (b_v, m_v_load)]
                    prog.comp("tensor", c_t,
                              lambda e, dg=dg, cch=cch: e.matmul(
                                  pband[4][:], slotKV[:, cch * 2 * D + dg * 128: cch * 2 * D + (dg + 1) * 128],
                                  ka_bf[:, cch * LC:(cch + 1) * LC],
                                  start=(cch == 0), stop=(cch == G - 1)), waits=wz)
                m_vmm = c_t.n
                m_v_used = m_vmm
                m_ffn_v = prog.comp("vector", c_v,
                                    lambda e: e.tensor_tensor(ffn_g[:], rtmp[:], pband[4][:], ALU.mult),
                                    waits=[(c_t, m_vmm), (c_s, m_rsig), (c_t, m_ffn_tr)])
                m_ffn_v = vsettle()
                bank = pband[dg % 2]
                for j in range(NLT):
                    m_ffn_tr = prog.comp("tensor", c_t,
                                         lambda e, j=j, bank=bank: e.transpose(
                                             bank[:, j * 128:(j + 1) * 128],
                                             ffn_g[:, j * 128:(j + 1) * 128], ident_sb[:]),
                                         waits=[(c_v, m_ffn_v), (c_s, m_evS[dg % 2])])
                for j in range(NLT):
                    m_evS[dg % 2] = prog.comp(
                        "scalar", c_s,
                        lambda e, j=j, dg=dg, bank=bank: e.activation(
                            opart_t[:, j * D + dg * 128: j * D + (dg + 1) * 128],
                            bank[:, j * 128:(j + 1) * 128], AF.Copy, bias=0.0, scale=1.0),
                        waits=[(c_t, m_ffn_tr), (c_v, m_opadd[j])]
                        + ([m_opst_b] if m_opst_b else []))
                    m_opev[j][dg] = m_evS[dg % 2]
                if dg == G - 1:
                    mset = ssettle()
                    for j in range(NLT):
                        m_opev[j][dg] = mset

            for cch in range(G):
                wz = [(c_v, m_xcvcast[cch])]
                if cch == 0:
                    wz.append((c_v, m_zfree))
                    if m_convfree[0]:
                        wz.append(m_convfree[0])
                prog.comp("tensor", c_t,
                          lambda e, cch=cch: e.matmul(
                              pband[5][0:64, :], wdt1_bf[:, cch * R:(cch + 1) * R],
                              xcv_bf[:, cch * LC:(cch + 1) * LC],
                              start=(cch == 0), stop=(cch == G - 1)), waits=wz)
            m_zmm = c_t.n
            mze = prog.comp("vector", c_v, lambda e: e.tensor_copy(zev_sb[:], pband[5][0:64, :]),
                            waits=[(c_t, m_zmm)])
            m_zfree = mze
            m_zcast_b[b] = prog.comp("vector", c_v,
                                     lambda e, b=b, l0=l0: e.tensor_copy(
                                         zbf_sb[:, b * L + l0: b * L + l0 + LC], zev_sb[:]))
            m_zcast_b[b] = vsettle()
            for cch in range(G):
                wz = [(c_v, m_xcvcast[cch])]
                if cch == 0:
                    wz.append((c_v, m_bfree))
                    if m_convfree[1]:
                        wz.append(m_convfree[1])
                prog.comp("tensor", c_t,
                          lambda e, cch=cch: e.matmul(
                              pband[6][0:16, :], wb_bf[:, cch * N:(cch + 1) * N],
                              xcv_bf[:, cch * LC:(cch + 1) * LC],
                              start=(cch == 0), stop=(cch == G - 1)), waits=wz)
            m_bmm = c_t.n
            prog.comp("vector", c_v, lambda e: e.tensor_copy(bev_sb[:], pband[6][0:16, :]),
                      waits=[(c_t, m_bmm)] + ([m_bsp_b] if m_bsp_b else []))
            m_bfree = vsettle()
            bsp = dbatch()
            msp = prog.dma("gpsimd", bsp, b_sc[b][:, l0:l0 + LC], bev_sb[:],
                           waits=[(c_v, m_bfree)])
            m_bsp_b = (bsp, msp)
            for cch in range(G):
                wz = [(c_v, m_xcvcast[cch])]
                if cch == 0:
                    wz.append((c_v, m_cfree))
                    if m_convfree[2]:
                        wz.append(m_convfree[2])
                prog.comp("tensor", c_t,
                          lambda e, cch=cch: e.matmul(
                              pband[7][0:16, :], wc_bf[:, cch * N:(cch + 1) * N],
                              xcv_bf[:, cch * LC:(cch + 1) * LC],
                              start=(cch == 0), stop=(cch == G - 1)), waits=wz)
            m_cmm = c_t.n
            prog.comp("vector", c_v, lambda e: e.tensor_copy(cev_sb[:], pband[7][0:16, :]),
                      waits=[(c_t, m_cmm)] + ([m_csp_b] if m_csp_b else []))
            m_cfree = vsettle()
            bsp = dbatch()
            msp = prog.dma("gpsimd", bsp, c_sc[b][:, l0:l0 + LC], cev_sb[:],
                           waits=[(c_v, m_cfree)])
            m_csp_b = (bsp, msp)

            for lt in range(NLT):
                m_opadd[lt] = prog.comp("vector", c_v,
                                        lambda e, lt=lt, xoff=xoff: e.tensor_tensor(
                                            opart_t[:, lt * D:(lt + 1) * D],
                                            opart_t[:, lt * D:(lt + 1) * D],
                                            x_ld[:, xoff + lt * D: xoff + (lt + 1) * D], ALU.add),
                                        waits=[(c_s, m_opev[lt][dgx]) for dgx in range(G)])
                m_opadd[lt] = vsettle()
                m_xfree[xpar] = m_opadd[lt]
                if lt == 0:
                    b_opst = dbatch()
                mop = prog.dma("gpsimd", b_opst, opart_sc[b, l0 + lt * 128: l0 + (lt + 1) * 128, :],
                               opart_t[:, lt * D:(lt + 1) * D], waits=[(c_v, m_opadd[lt])])
                m_opst_b = (b_opst, mop)

    bar = [(c, c.n) for c in ctrs] + [(c, c.n) for c in dpool]
    for eng in ("sync", "vector", "scalar", "tensor", "gpsimd"):
        prog.op(eng, lambda e: e.nop(), waits=[(c, v) for (c, v) in bar if v > 0], inc=None)

    m_bcld = [None, None]
    m_bcmm = [0, 0]
    m_bcev = [0, 0, 0, 0]
    m_bcuse_v = [0, 0]
    m_bcuse_p = [0, 0]
    m_dtmm = [0, 0]
    m_dtfree_v = [0, 0]
    m_dtfree_s = [0, 0]
    m_spl_v = 0
    m_spl_s = 0
    m_dtc = {}
    m_abar = {}
    m_tts = {}
    m_bx = {}
    m_yacc_eng = {}
    m_yeng_g = [None] * G
    m_yclip_eng = [None] * G
    m_carry = {}
    m_hfree = {}
    m_xcvld = 0
    m_cxmark = [0] * G
    m_dxmark = [0] * G
    m_yadd = [0] * G
    m_clip = [0] * G
    m_ytr = [0, 0]
    m_ytr_g = [0] * G
    b_opld_g = [None, None]
    m_ost_b = [None, None]
    m_oadd = [0, 0]
    m_ost = [0, 0]
    m_opldm = [0, 0]
    m_yacc_lastw = [0] * G
    plane = 0

    for b in (range(NB) if run_c else ()):
        for lq in range(NLQ):
            l0 = lq * LQ
            wxc = [(c_s, max(m_dxmark)), (c_v, max(m_cxmark))] if (b + lq) else []
            b_xcv = dbatch()
            for g in range(G):
                m_xcvld = prog.dma("sync", b_xcv, xcv_c[:, g * LQ:(g + 1) * LQ],
                                   xcv_sc[b, g][:, l0:l0 + LQ], waits=wxc if g == 0 else [])
            def load_quarter(q, b=b, l0=l0):
                par = q % 2
                bq = dbatch()
                prog.dma("sync", bq, bcf_sb[0:1, par * 4096: par * 4096 + 2048],
                         b_sc[b][q * 4:(q + 1) * 4, l0:l0 + LQ], waits=[(c_t, m_bcmm[par])])
                mq = prog.dma("sync", bq, bcf_sb[0:1, par * 4096 + 2048: (par + 1) * 4096],
                              c_sc[b][q * 4:(q + 1) * 4, l0:l0 + LQ])
                m_bcld[par] = (bq, mq)
            load_quarter(0)
            load_quarter(1)

            for g in (range(G) if csub >= 1 else ()):
                par = g % 2
                bank = pband[4 + par]
                wz = [(c_v, m_dtfree_v[par]), (c_s, m_dtfree_s[par])]
                prog.comp("tensor", c_t,
                          lambda e, g=g, b=b, l0=l0, bank=bank: e.matmul(
                              bank[:], wdt2_bf[:, g * 128:(g + 1) * 128],
                              zbf_sb[:, b * L + l0: b * L + l0 + LQ],
                              start=True, stop=True), waits=wz)
                m_dtmm[par] = c_t.n
                bcol = bdt_sb[:, g:g + 1]
                mts = prog.comp("vector", c_v,
                                lambda e, bank=bank, bcol=bcol: e.tensor_scalar(
                                    sp3[:], bank[:], bcol, 0.0, ALU.add, ALU.max),
                                waits=[(c_t, m_dtmm[par]), (c_s, m_dtfree_s[par])])
                m_dtfree_v[par] = mts
                prog.comp("scalar", c_s,
                          lambda e, bank=bank, bcol=bcol: e.activation(sp1[:], bank[:], AF.Abs, bias=bcol, scale=1.0),
                          waits=[(c_t, m_dtmm[par]), (c_v, mts), (c_v, m_spl_v)])
                prog.comp("scalar", c_s, lambda e: e.activation(sp2[:], sp1[:], AF.Exp, bias=0.0, scale=-1.0))
                prog.comp("scalar", c_s, lambda e: e.activation(sp1[:], sp2[:], AF.Ln, bias=1.0, scale=1.0))
                m_spl_s = ssettle()
                m_dtfree_s[par] = m_spl_s
                m_spl_v = prog.comp("vector", c_v,
                                    lambda e, g=g: e.tensor_tensor(dt_sb[:, g * LQ:(g + 1) * LQ], sp3[:], sp1[:], ALU.add),
                                    waits=[(c_s, m_spl_s)])
                m_cxmark[g] = prog.comp("vector", c_v,
                                        lambda e, g=g: e.scalar_tensor_tensor(
                                            cx_sb[:, g * LQ:(g + 1) * LQ], dt_sb[:, g * LQ:(g + 1) * LQ],
                                            1.0, xcv_c[:, g * LQ:(g + 1) * LQ], ALU.min, ALU.mult),
                                        waits=[(b_xcv, m_xcvld)])

            def emit_dtc(pl, plane0=None):
                nn, gg = divmod(pl - _pl0[0], G)
                if nn >= N:
                    return
                ppar = pl % 2
                col = gg * N + nn
                prog.comp("vector", c_v,
                          lambda e, gg=gg, col=col, ppar=ppar: e.tensor_scalar(
                              dtc_sb[:, ppar * LQ:(ppar + 1) * LQ],
                              dt_sb[:, gg * LQ:(gg + 1) * LQ],
                              lo_sb[:, col:col + 1], hi_sb[:, col:col + 1],
                              ALU.max, ALU.min),
                          waits=[(c_s, m_abar.get(pl - 2, 0)), (c_v, m_spl_v) if nn == 0 else (c_v, 0)])
                m_dtc[pl] = vsettle()

            def emit_abar(pl, plane0=None):
                nn, gg = divmod(pl - _pl0[0], G)
                if nn >= N:
                    return
                ppar = pl % 2
                col = gg * N + nn
                prog.comp("scalar", c_s,
                          lambda e, col=col, ppar=ppar: e.activation(
                              abar_sb[:, ppar * LQ:(ppar + 1) * LQ],
                              dtc_sb[:, ppar * LQ:(ppar + 1) * LQ],
                              AF.Exp, bias=0.0, scale=A_sb[:, col:col + 1]),
                          waits=[(c_v, m_dtc[pl]), (c_v, m_tts.get(pl - 2, 0))])
                m_abar[pl] = ssettle()

            _pl0 = [plane]
            import os as _os2
            _lead = int(_os2.environ.get("DTLEAD", "2"))
            if _lead:
                emit_dtc(plane)
                emit_abar(plane)
                emit_dtc(plane + 1)
                emit_abar(plane + 1)
            for n in (range(N) if csub >= 2 else ()):
                npar = n % 2
                q = n // 4
                j = n % 4
                qpar = q % 2
                prog.comp("tensor", c_t,
                          lambda e, npar=npar, qpar=qpar, j=j: e.matmul(
                              pband[npar][:], ones_sb[:],
                              bcf_sb[0:1, qpar * 4096 + j * LQ: qpar * 4096 + (j + 1) * LQ],
                              start=True, stop=True),
                          waits=[m_bcld[qpar], (c_s, m_bcev[npar])])
                mbB = c_t.n
                prog.comp("tensor", c_t,
                          lambda e, npar=npar, qpar=qpar, j=j: e.matmul(
                              pband[2 + npar][:], ones_sb[:],
                              bcf_sb[0:1, qpar * 4096 + 2048 + j * LQ: qpar * 4096 + 2048 + (j + 1) * LQ],
                              start=True, stop=True),
                          waits=[(c_s, m_bcev[2 + npar])])
                mbC = c_t.n
                m_bcmm[qpar] = mbC
                m_bcev[npar] = prog.comp("scalar", c_s,
                                         lambda e, npar=npar: e.activation(
                                             Bb_sb[:, npar * LQ:(npar + 1) * LQ], pband[npar][:],
                                             AF.Copy, bias=0.0, scale=1.0),
                                         waits=[(c_t, mbB), (c_v, m_bcuse_v[npar])])
                m_bcev[2 + npar] = prog.comp("scalar", c_s,
                                             lambda e, npar=npar: e.activation(
                                                 Cb_sb[:, npar * LQ:(npar + 1) * LQ], pband[2 + npar][:],
                                                 AF.Copy, bias=0.0, scale=1.0),
                                             waits=[(c_t, mbC), (c_v, m_bcuse_v[npar])])
                m_bcset = ssettle()
                m_bcev_mark = (m_bcset, m_bcset)

                for g in range(G):
                    ppar = plane % 2
                    ccol = b * G * N + g * N + n
                    prog.comp("vector", c_v,
                              lambda e, g=g, npar=npar: e.tensor_tensor(
                                  bx_sb[:, g * LQ:(g + 1) * LQ],
                                  cx_sb[:, g * LQ:(g + 1) * LQ],
                                  Bb_sb[:, npar * LQ:(npar + 1) * LQ], ALU.mult),
                              waits=[(c_s, m_bcev_mark[0])])
                    if not _lead:
                        emit_dtc(plane)
                        emit_abar(plane)
                    init = 0.0 if lq == 0 else hcarry[:, ccol:ccol + 1]
                    m_tts[plane] = prog.comp("vector", c_v,
                                             lambda e, ppar=ppar, init=init, g=g: e.tensor_tensor_scan(
                                                 h_sb[:, ppar * LQ:(ppar + 1) * LQ],
                                                 abar_sb[:, ppar * LQ:(ppar + 1) * LQ],
                                                 bx_sb[:, g * LQ:(g + 1) * LQ],
                                                 init, ALU.mult, ALU.add),
                                             waits=[(c_s, m_abar[plane]),
                                                    (c_p, m_carry.get((b, g, n), 0)),
                                                    (c_p, m_hfree.get(plane - 2, 0))])
                    m_carry[(b, g, n)] = prog.comp("gpsimd", c_p,
                                                   lambda e, ppar=ppar, ccol=ccol: e.tensor_copy(
                                                       hcarry[:, ccol:ccol + 1],
                                                       h_sb[:, ppar * LQ + LQ - 1: ppar * LQ + LQ]),
                                                   waits=[(c_v, m_tts[plane])])
                    m_hfree[plane] = m_carry[(b, g, n)]
                    if _lead:
                        emit_dtc(plane + 2)
                        emit_abar(plane + 2)
                    if n == 0:
                        m_yacc_lastw[g] = prog.comp("vector", c_v,
                                                    lambda e, g=g, ppar=ppar, npar=npar: e.tensor_tensor(
                                                        yacc_sb[:, g * LQ:(g + 1) * LQ],
                                                        h_sb[:, ppar * LQ:(ppar + 1) * LQ],
                                                        Cb_sb[:, npar * LQ:(npar + 1) * LQ], ALU.mult),
                                                    waits=[(c_s, m_bcev_mark[1]), (c_t, m_ytr_g[g])])
                    else:
                        prog.comp("vector", c_v,
                                  lambda e, ppar=ppar, npar=npar: e.tensor_tensor(
                                      hc_sb[:, 0:LQ],
                                      h_sb[:, ppar * LQ:(ppar + 1) * LQ],
                                      Cb_sb[:, npar * LQ:(npar + 1) * LQ], ALU.mult),
                                  waits=[(c_s, m_bcev_mark[1])])
                        m_yacc_lastw[g] = prog.comp("vector", c_v,
                                                    lambda e, g=g: e.tensor_tensor(
                                                        yacc_sb[:, g * LQ:(g + 1) * LQ],
                                                        yacc_sb[:, g * LQ:(g + 1) * LQ],
                                                        hc_sb[:, 0:LQ], ALU.add))
                    m_yeng_g[g] = (c_v, m_yacc_lastw[g])
                    m_carry[(b, g, n)] = prog.comp("gpsimd", c_p,
                                                   lambda e, ppar=ppar, ccol=ccol: e.tensor_copy(
                                                       hcarry[:, ccol:ccol + 1],
                                                       h_sb[:, ppar * LQ + LQ - 1: ppar * LQ + LQ]),
                                                   waits=[(c_v, m_yacc_lastw[g])])
                    m_hfree[plane] = m_carry[(b, g, n)]
                    plane += 1
                m_bcuse_v[npar] = c_v.n
                if n % 4 == 3 and n // 4 + 2 <= 3:
                    load_quarter(n // 4 + 2)

            def emit_out_stores(gg, b=b, l0=l0):
                gp = gg % 2
                bst = dbatch()
                mst = 0
                for lt in range(NLT):
                    w0 = [(c_v, m_oadd[gp])] if lt == 0 else []
                    mst = prog.dma("sync", bst,
                                   out_d[b, l0 + lt * 128: l0 + (lt + 1) * 128, gg * 128:(gg + 1) * 128],
                                   ostg[:, gp * LQ + lt * 128: gp * LQ + (lt + 1) * 128], waits=w0)
                m_ost_b[gp] = (bst, mst)
            for g in (range(G) if csub >= 3 else ()):
                gpar = g % 2
                yeng, yctr = ("vector", c_v)
                b_opld = dbatch()
                for lt in range(NLT):
                    m_opldm[gpar] = prog.dma("sync", b_opld,
                                             opld[:, gpar * LQ + lt * 128: gpar * LQ + (lt + 1) * 128],
                                             opart_sc[b, l0 + lt * 128: l0 + (lt + 1) * 128,
                                                      g * 128:(g + 1) * 128],
                                             waits=[(c_v, m_oadd[gpar])] if lt == 0 else [])
                b_opld_g[gpar] = b_opld
                prog.comp("scalar", c_s,
                          lambda e, g=g: e.activation(
                              dx_sb[:], xcv_c[:, g * LQ:(g + 1) * LQ], AF.Copy,
                              bias=0.0, scale=dcol_sb[:, g:g + 1]),
                          waits=[m_yclip_eng[(g - 1) % G]] if m_yclip_eng[(g - 1) % G] else [])
                m_dxmark[g] = ssettle()

                prog.comp(yeng, yctr,
                          lambda e, g=g: e.tensor_tensor(
                              yacc_sb[:, g * LQ:(g + 1) * LQ],
                              yacc_sb[:, g * LQ:(g + 1) * LQ], dx_sb[:], ALU.add),
                          waits=[(c_s, m_dxmark[g]), m_yeng_g[g]])
                prog.comp(yeng, yctr,
                          lambda e, g=g: e.tensor_scalar(
                              yacc_sb[:, g * LQ:(g + 1) * LQ],
                              yacc_sb[:, g * LQ:(g + 1) * LQ],
                              -50.0, 50.0, ALU.max, ALU.min))
                mclip = vsettle()
                m_yclip_eng[g] = (yctr, mclip)
                bank = pband[6 + gpar]
                for j in range(NLT):
                    m_ytr[gpar] = prog.comp("tensor", c_t,
                                            lambda e, g=g, j=j, bank=bank: e.transpose(
                                                bank[:, j * 128:(j + 1) * 128],
                                                yacc_sb[:, g * LQ + j * 128: g * LQ + (j + 1) * 128],
                                                ident_sb[:]),
                                            waits=[m_yclip_eng[g], (c_v, m_oadd[gpar])])
                m_ytr_g[g] = m_ytr[gpar]
                for j in range(NLT):
                    m_oadd[gpar] = prog.comp("vector", c_v,
                                             lambda e, j=j, gpar=gpar, bank=bank: e.tensor_tensor(
                                                 ostg[:, gpar * LQ + j * 128: gpar * LQ + (j + 1) * 128],
                                                 bank[:, j * 128:(j + 1) * 128],
                                                 opld[:, gpar * LQ + j * 128: gpar * LQ + (j + 1) * 128],
                                                 ALU.add),
                                             waits=[(c_t, m_ytr[gpar]),
                                                    (b_opld_g[gpar], m_opldm[gpar])]
                                             + ([m_ost_b[gpar]] if m_ost_b[gpar] else []))
                m_oadd[gpar] = vsettle()
                if g >= 1:
                    emit_out_stores(g - 1)
            if csub >= 3:
                emit_out_stores(G - 1)

    fin = [(c, c.n) for c in ctrs] + [(c, c.n) for c in dpool]
    for eng in ("gpsimd", "sync"):
        prog.op(eng, lambda e: e.nop(), waits=[(c, v) for (c, v) in fin if v > 0])

    from contextlib import ExitStack
    with ExitStack() as stk:
        for i, c in enumerate(ctrs):
            c.h = stk.enter_context(nc.semaphore(f"sem{i}"))
        for i, c in enumerate(dpool):
            c.h = stk.enter_context(nc.semaphore(f"dsem{i}"))
        blk = stk.enter_context(nc.Block())
        items = prog.items

        def mk(name):
            def emit(eng):
                seen = {}
                for (en, fn, waits, inc) in items:
                    if en != name:
                        continue
                    for (ctr, val) in waits:
                        k = id(ctr)
                        if seen.get(k, 0) < val:
                            eng.wait_ge(ctr.h, val)
                            seen[k] = val
                    inst = fn(eng)
                    if inc is not None:
                        inst.then_inc(inc[0].h, inc[1])
            return emit

        blk.sync(mk("sync"))
        blk.vector(mk("vector"))
        blk.scalar(mk("scalar"))
        blk.tensor(mk("tensor"))
        blk.gpsimd(mk("gpsimd"))

    for (en, fn, waits, inc) in prog.items:
        for (ctr, val) in waits:
            assert val <= ctr.n, f"wait {val} exceeds final {ctr.n} on {en}"
    return nc


def _conv_diag(conv_w):
    cw = conv_w[:, 0, :]
    cd = np.zeros((128, G * 4 * 128), np.float32)
    idx = np.arange(128)
    for g in range(G):
        for k in range(4):
            cd[idx, (g * 4 + k) * 128 + idx] = cw[g * 128:(g + 1) * 128, k]
    return cd


def _layout_inputs(inputs, core):
    i = inputs
    f32 = np.float32

    def percol(v):
        return np.ascontiguousarray(v.reshape(G, 128).T).astype(f32, copy=False)

    m = {
        "x": np.ascontiguousarray(i["L0_attn_x"][2 * core:2 * core + 2]),
        "wrT": np.ascontiguousarray(i["L0_ffn_W_r"].T),
        "wkT": np.ascontiguousarray(i["L0_ffn_W_k"].T),
        "wvT": np.ascontiguousarray(i["L0_ffn_W_v"].T),
        "wdt1T": np.ascontiguousarray(i["L0_attn_W_dt1"].T),
        "wdt2T": np.ascontiguousarray(i["L0_attn_W_dt2"].T),
        "wbT": np.ascontiguousarray(i["L0_attn_W_B"].T),
        "wcT": np.ascontiguousarray(i["L0_attn_W_C"].T),
        "alogp": np.ascontiguousarray(
            i["L0_attn_A_log"].reshape(G, 128, N).transpose(1, 0, 2).reshape(128, G * N)),
        "dcolp": percol(i["L0_attn_D_param"]),
        "bdtp": percol(i["L0_attn_b_dt"]),
        "cbp": percol(i["L0_attn_conv_b"]),
        "murp": percol(i["L0_ffn_mu_r"]),
        "mukp": percol(i["L0_ffn_mu_k"]),
        "convwp": np.ascontiguousarray(
            i["L0_attn_conv_w"][:, 0, :].reshape(G, 128, 4).transpose(1, 0, 2).reshape(128, G * 4)),
        "convdg": _conv_diag(i["L0_attn_conv_w"]),
        "onesr": np.ones((1, 128), f32),
        "kconst": np.tile(np.array([[C_LO, C_HI, EPS, -0.5]], f32), (128, 1)),
        "identm": np.eye(128, dtype=f32),
    }
    return m


_nc_cache = {}


def kernel(**inputs):
    key = "default"
    if key not in _nc_cache:
        _nc_cache[key] = build_program(taps=_TAPS)
    nc = _nc_cache[key]
    in_maps = [_layout_inputs(inputs, core) for core in range(NCORES)]
    res = run_bass_kernel_spmd(nc, in_maps, core_ids=list(range(NCORES)))
    out = np.concatenate([res.results[c]["out"] for c in range(NCORES)], axis=0)
    return out
